# version 11
# baseline (speedup 1.0000x reference)
import contextlib
import numpy as np
import concourse.bass as bass
import concourse.mybir as mybir
from concourse.bass_utils import run_bass_kernel_spmd
from concourse.aot_env import allow_ici_all_to_all

F32 = mybir.dt.float32
BF16 = mybir.dt.bfloat16
FP8 = mybir.dt.float8e4
U8 = mybir.dt.uint8
AF = mybir.ActivationFunctionType
A = mybir.AluOpType
AX = mybir.AxisListType.X

P = 128
SC = 1024
D = 2048
NH = 16
DH = 128
FF = 6144
NC8 = 8
C_RNE = 12582912.0
EPS = 1e-5
RG = [list(range(NC8))]
INV_D = 1.0 / D
SQ128 = float(np.sqrt(128.0))

SEMS = ["sx", "sl", "sg", "sq", "sc_", "sv", "st_", "so", "sn", "ss",
        "sea", "seb", "sla", "slb_", "sw0", "sw1", "sw2", "sw3",
        "sw4", "sw5", "sw6", "sw7", "sqa", "sqb", "scc",
        "swa", "swb", "sba", "sbb", "sfa", "sfb"]
ENGS = ["sync", "scalar", "gpsimd", "vector", "tensor"]


class Ctx:
    def __init__(self, me, eng, sems):
        self.me = me
        self.eng = eng
        self.sems = sems
        self.ctr = {n: 0 for n in SEMS}

    def op(self, engine, fn, inc=None):
        if self.me == engine:
            inst = fn(self.eng)
            if inc is not None:
                inst.then_inc(self.sems[inc[0]], inc[1])
        if inc is not None:
            self.ctr[inc[0]] += inc[1]

    def wait(self, engine, sem, lag=0, value=None):
        v = (self.ctr[sem] - lag) if value is None else value
        if v <= 0:
            return
        if self.me == engine:
            self.eng.wait_ge(self.sems[sem], v)

    def mark(self, sem):
        return self.ctr[sem]

    def chain(self, engine, fn):
        self.op(engine, fn, inc=("ss", 1))
        self.wait(engine, "ss")


def build_nc(dbg=False, phases=99):
    nc = bass.Bass("TRN2", target_bir_lowering=False)

    x_in = nc.dram_tensor("x", [SC, D], F32, kind="ExternalInput").ap()
    wqkv = nc.dram_tensor("wqkv", [D, 3 * D], F32, kind="ExternalInput").ap()
    wout = nc.dram_tensor("wout", [D, D], F32, kind="ExternalInput").ap()
    bout = nc.dram_tensor("bout", [1, D], F32, kind="ExternalInput").ap()
    gw = nc.dram_tensor("gw", [D, FF], F32, kind="ExternalInput").ap()
    uw = nc.dram_tensor("uw", [D, FF], F32, kind="ExternalInput").ap()
    dw = nc.dram_tensor("dw", [FF, D], F32, kind="ExternalInput").ap()
    gsv = nc.dram_tensor("gsv", [1, FF], F32, kind="ExternalInput").ap()
    usv = nc.dram_tensor("usv", [1, FF], F32, kind="ExternalInput").ap()
    dsv = nc.dram_tensor("dsv", [1, D], F32, kind="ExternalInput").ap()
    cvec = nc.dram_tensor("cvec", [1, D], F32, kind="ExternalInput").ap()
    wmod = nc.dram_tensor("wmod", [D, 1536], F32, kind="ExternalInput").ap()
    y_out = nc.dram_tensor("y", [SC, D], F32, kind="ExternalOutput").ap()

    mod_in = nc.dram_tensor("mod_in", [1, 1536], F32)
    mod_out = nc.dram_tensor("mod_out", [NC8, 1536], F32, addr_space="Shared")
    mx_in = nc.dram_tensor("mx_in", [P, 4], F32)
    mx_out = nc.dram_tensor("mx_out", [P, 4], F32, addr_space="Shared")
    kn_in = nc.dram_tensor("kn_in", [1, NH], F32)
    kn_out = nc.dram_tensor("kn_out", [1, NH], F32, addr_space="Shared")
    qkv_sc = nc.dram_tensor("qkv_sc", [3 * D, SC], F32)
    k8_in = nc.dram_tensor("k8_in", [D, SC], BF16)
    k8_out = nc.dram_tensor("k8_out", [NC8 * D, SC], BF16, addr_space="Shared")
    v8_in = nc.dram_tensor("v8_in", [NH * SC, DH], BF16)
    v8_out = nc.dram_tensor("v8_out", [NC8 * NH * SC, DH], BF16, addr_space="Shared")
    x2_sc = nc.dram_tensor("x2_sc", [SC, D], F32)
    g8_sc = nc.dram_tensor("g8_sc", [48, P, D], FP8)
    u8_sc = nc.dram_tensor("u8_sc", [48, P, D], FP8)
    d8_sc = nc.dram_tensor("d8_sc", [48, P, D], FP8)

    DB = {}
    if dbg:
        for nm, shape, dt in [
            ("d_mod", [NC8, 1536], F32), ("d_h1T", [P, NH * SC], BF16),
            ("d_qkv", [3 * D, SC], F32), ("d_mx", [1, 32], F32), ("d_rmax", [P, 8], F32), ("d_stat", [P, 64], F32),
            ("d_q8T", [P, NH * SC], BF16), ("d_k8o", [NC8 * D, SC], BF16),
            ("d_v8o", [NC8 * NH * SC, DH], BF16), ("d_negc", [1, 64], F32),
            ("d_v8i", [NH * SC, DH], BF16), ("d_v8late", [NC8 * NH * SC, DH], BF16),
            ("d_aoT", [P, NH * SC], BF16), ("d_x2", [SC, D], F32),
            ("d_h2T", [P, NH * SC], BF16),
        ]:
            DB[nm] = nc.dram_tensor(nm, shape, dt, kind="ExternalOutput").ap()

    O_HT, O_Q8T, O_AOT, O_XN, O_POOL = 0, 32768, 65536, 98304, 131072
    NB_POOL = 65536
    arena = nc.alloc_sbuf_tensor("arena", [P, O_POOL + NB_POOL], U8).ap()

    def view(off, dtype, cols, rows=P):
        return arena[0:rows, off:off + cols * mybir.dt.size(dtype)].bitcast(dtype)

    hT = view(O_HT, BF16, NH * SC)
    q8T = view(O_Q8T, BF16, NH * SC)
    aoT = view(O_AOT, BF16, NH * SC)
    xn = view(O_XN, F32, 8192)
    fuseT = view(O_Q8T, BF16, 3 * NH * SC)
    k8h = [view(O_XN, BF16, 8192), view(O_XN + 16384, BF16, 8192)]
    gaB = view(O_Q8T, F32, D)
    gabB = view(O_Q8T + 8192, F32, D)
    dsb2B = view(O_HT, F32, D)
    fx2all = view(O_HT + 8192, F32, 4096)
    dscr = view(O_HT + 24576, F32, 2048)

    po = [O_POOL]

    def pool(dtype, cols, rows=P, align=64):
        nbytes = cols * mybir.dt.size(dtype)
        a = (po[0] + align - 1) // align * align
        ap = arena[0:rows, a:a + nbytes].bitcast(dtype)
        po[0] = a + nbytes
        return ap

    ident32 = pool(F32, 128)
    ident16 = pool(BF16, 128)
    ones1 = pool(F32, 128, rows=1)
    onescol = pool(BF16, 1)
    sa_c = pool(F32, NH)
    ca1_c = pool(F32, NH)
    sb_c = pool(F32, NH)
    cb1_c = pool(F32, NH)
    gscol = pool(F32, 48)
    uscol = pool(F32, 48)
    rmax = pool(F32, 4)
    statb = pool(F32, 8)
    constsA = pool(F32, 16)
    negcB = pool(F32, NH)
    siluc = pool(F32, NH)
    rmxt = pool(F32, 1)
    rowbuf = pool(F32, 2048, rows=1)
    rowb2 = pool(F32, 512, rows=1)
    cimm = pool(F32, 4)
    statdump = pool(F32, 64)
    PERS_END = po[0]

    def pview(off, dtype, cols, rows=P):
        a = PERS_END + off
        nbytes = cols * mybir.dt.size(dtype)
        assert a + nbytes <= O_POOL + NB_POOL, f"pool overflow {a + nbytes}"
        return arena[0:rows, a:a + nbytes].bitcast(dtype)

    wmodb = [pview(0, F32, 1536), pview(6144, F32, 1536)]
    wqb = [pview(0, BF16, 2048), pview(4096, BF16, 2048)]
    wqb3 = [pview(24576 + j * 4096, BF16, 2048) for j in range(3)]
    stage = [pview(8192, F32, 1024), pview(12288, F32, 1024)]
    scrap = pview(16384, F32, 2048)
    wst = [view(O_XN + j * 8192, F32, 2048) for j in range(4)]
    fp8buf = pview(16384, FP8, 1024)
    mainb = pview(17408, BF16, 1024)
    fixb = pview(19456, BF16, 1024)
    zb = pview(21504, F32, 1024)
    sqb = pview(25600, F32, 1024)
    mskb = pview(29696, U8, 1024)
    k8stage = pview(30720, BF16, 1024)
    v8q = pview(32768, BF16, 1024)
    v8stage = pview(34816, BF16, 1024)
    sqnb = pview(36864, BF16, 1024)
    v8h = [pview(0, BF16, 8256), pview(16512, BF16, 8256)]
    PT = [pview(33024, BF16, 1024), pview(35072, BF16, 1024)]
    attnstage = pview(37120, BF16, 512)
    rl4 = pview(38144, F32, 4)
    woutb = [pview(0, BF16, 2048), pview(4096, BF16, 2048)]
    woutb8 = [pview(i * 1024, BF16, 512) for i in range(8)]
    xsl = [pview(8192, F32, 512), pview(10240, F32, 512)]
    epis = pview(12288, F32, 512)
    x2b = [pview(14336, F32, 512), pview(18432, F32, 512)]
    opscr = pview(16384, F32, 2048)
    gwb3 = [pview(0, BF16, 2048), pview(4096, BF16, 2048), pview(8192, BF16, 2048)]
    uwb3 = [pview(12288, BF16, 2048), pview(16384, BF16, 2048), pview(20480, BF16, 2048)]
    sgbuf = pview(24576, F32, 1024)
    fepi = [pview(28672, F32, 512), pview(30720, F32, 512)]
    dwb6 = [pview(32768 + j * 1024, BF16, 512) for j in range(6)]

    ps1 = nc.alloc_psum_tensor("psA", [P, 2048], F32).ap()
    ps2 = nc.alloc_psum_tensor("psB", [P, 2048], F32).ap()
    psT = ps1[:, 1024:1536].bitcast(BF16)

    stack = contextlib.ExitStack()
    sems = {n: stack.enter_context(nc.semaphore(n)) for n in SEMS}

    HIQ, HIK, HIV, I5Q, I5K, I5V, THQ, THK, THV, CEXP, VSC, NEGC = range(12)

    def script(c: Ctx):
        c.op("gpsimd", lambda g: g.memset(ident32[:], 0.0))
        c.op("gpsimd", lambda g: g.affine_select(
            out=ident32[:], in_=ident32[:], compare_op=A.not_equal, fill=1.0,
            base=0, pattern=[[-1, 128]], channel_multiplier=1))
        c.op("gpsimd", lambda g: g.memset(ident16[:], 0.0))
        c.op("gpsimd", lambda g: g.affine_select(
            out=ident16[:], in_=ident16[:], compare_op=A.not_equal, fill=1.0,
            base=0, pattern=[[-1, 128]], channel_multiplier=1))
        c.op("gpsimd", lambda g: g.memset(ones1[:], 1.0))
        c.op("gpsimd", lambda g: g.memset(cimm[:, 0:1], EPS))
        c.op("gpsimd", lambda g: g.memset(cimm[:, 1:2], 2.0))
        c.op("gpsimd", lambda g: g.memset(onescol[:], 1.0), inc=("sq", 1))

        def dmajor(row_ap, n):
            return row_ap.rearrange("o (j p) -> (o p) j", p=P)

        c.op("sync", lambda s: s.dma_start(out=siluc[:], in_=dmajor(cvec, NH), allow_slow_non_contiguous=True), inc=("sx", 16))
        c.op("sync", lambda s: s.dma_start(out=gscol[:], in_=dmajor(gsv, 48), allow_slow_non_contiguous=True), inc=("sx", 16))
        c.op("sync", lambda s: s.dma_start(out=uscol[:], in_=dmajor(usv, 48), allow_slow_non_contiguous=True), inc=("sx", 16))
        c.wait("scalar", "sx")
        c.op("scalar", lambda s: s.activation(out=siluc[:], in_=siluc[:], func=AF.Silu),
             inc=("sc_", 1))
        c.op("vector", lambda v: v.memset(rmax[:], 0.0), inc=("sv", 1))

        for dc in range(2):
            c.op("sync", lambda s, dc=dc: s.dma_start(
                out=wmodb[dc % 2][:], in_=wmod[dc * P:(dc + 1) * P, :]),
                inc=("sea" if dc % 2 == 0 else "seb", 16))
        c.wait("tensor", "sc_")
        for dc in range(16):
            c.wait("tensor", "sea" if dc % 2 == 0 else "seb")
            for cb in range(3):
                dst = ps1[0:1, cb * 512:(cb + 1) * 512]
                c.op("tensor", lambda t, dc=dc, cb=cb, dst=dst: t.matmul(
                    out=dst, lhsT=siluc[:, dc:dc + 1],
                    rhs=wmodb[dc % 2][:, cb * 512:(cb + 1) * 512],
                    start=(dc == 0), stop=(dc == 15)),
                    inc=("st_", 1) if cb == 2 else None)
            if dc + 2 < 16:
                c.wait("sync", "st_")
                c.op("sync", lambda s, dc=dc: s.dma_start(
                    out=wmodb[dc % 2][:], in_=wmod[(dc + 2) * P:(dc + 3) * P, :]),
                    inc=("sea" if dc % 2 == 0 else "seb", 16))
        c.wait("vector", "st_")
        c.op("vector", lambda v: v.tensor_copy(out=rowbuf[0:1, 0:1536], in_=ps1[0:1, 0:1536]),
             inc=("sv", 1))
        c.wait("sync", "sv")
        c.op("sync", lambda s: s.dma_start(out=mod_in.ap(), in_=rowbuf[0:1, 0:1536]),
             inc=("sx", 16))
        c.wait("gpsimd", "sx")
        c.op("gpsimd", lambda g: g.collective_compute(
            "AllGather", A.bypass, replica_groups=RG,
            ins=[mod_in.ap().opt()], outs=[mod_out.ap().opt()]), inc=("scc", 1))
        c.wait("sync", "scc")
        modflat = mod_out.ap().rearrange("a b -> (a b)").rearrange("(o a) -> o a", o=1)
        for i, dst in [(0, sa_c), (1, ca1_c), (3, sb_c), (4, cb1_c)]:
            c.op("sync", lambda s, i=i, dst=dst: s.dma_start(
                out=dst[:], in_=dmajor(modflat[0:1, i * D:(i + 1) * D], NH),
                allow_slow_non_contiguous=True), inc=("sx", 16))
        if dbg:
            c.op("sync", lambda s: s.dma_start(out=DB["d_mod"], in_=mod_out.ap()),
                 inc=("so", 16))
        c.wait("vector", "sx")
        c.op("vector", lambda v: v.tensor_scalar(
            out=ca1_c[:], in0=ca1_c[:], scalar1=1.0, scalar2=None, op0=A.add))
        c.op("vector", lambda v: v.tensor_scalar(
            out=cb1_c[:], in0=cb1_c[:], scalar1=1.0, scalar2=None, op0=A.add),
            inc=("sv", 1))
        modv_ready = c.mark("sv")

        def emit_LN(src_rows_fn, ca1v, sav, tap=None):
            for half in range(2):
                c.wait("sync", "st_")
                c.wait("sync", "sv")
                for sc2 in range(4):
                    sc = half * 4 + sc2
                    xsl_ = xn[:, sc2 * D:(sc2 + 1) * D]
                    c.op("sync", lambda s, sc=sc, xsl_=xsl_: s.dma_start(
                        out=xsl_, in_=src_rows_fn(sc)), inc=("sn", 16))
                    c.wait("vector", "sn")
                    c.op("vector", lambda v, xsl_=xsl_: v.reduce_sum(
                        out=statb[:, 0:1], in_=xsl_, axis=AX))
                    c.op("vector", lambda v, xsl_=xsl_: v.scalar_tensor_tensor(
                        out=scrap[:], in0=xsl_, scalar=0.0, in1=xsl_,
                        op0=A.add, op1=A.mult, accum_out=statb[:, 1:2]))
                    c.chain("vector", lambda v: v.tensor_scalar(
                        out=statb[:, 2:3], in0=statb[:, 0:1], scalar1=INV_D,
                        scalar2=None, op0=A.mult))
                    c.chain("vector", lambda v: v.tensor_tensor(
                        out=statb[:, 7:8], in0=statb[:, 2:3], in1=statb[:, 2:3], op=A.mult))
                    c.op("vector", lambda v: v.scalar_tensor_tensor(
                        out=statb[:, 3:4], in0=statb[:, 1:2], scalar=INV_D,
                        in1=statb[:, 7:8], op0=A.mult, op1=A.subtract), inc=("sv", 1))
                    c.wait("scalar", "sv")
                    c.wait("scalar", "sq", value=1)
                    c.op("scalar", lambda s: s.activation(
                        out=statb[:, 4:5], in_=statb[:, 3:4], func=AF.Sqrt,
                        bias=cimm[:, 0:1]), inc=("sc_", 1))
                    c.wait("vector", "sc_")
                    c.chain("vector", lambda v: v.reciprocal(out=statb[:, 5:6], in_=statb[:, 4:5]))
                    c.chain("vector", lambda v: v.tensor_tensor(
                        out=statb[:, 6:7], in0=statb[:, 2:3], in1=statb[:, 5:6], op=A.mult))
                    c.op("vector", lambda v, xsl_=xsl_: v.tensor_scalar(
                        out=xsl_, in0=xsl_, scalar1=statb[:, 5:6], scalar2=statb[:, 6:7],
                        op0=A.mult, op1=A.subtract), inc=("sv", 1))
                    if dbg and tap is not None:
                        c.op("vector", lambda v, sc=sc: v.tensor_copy(
                            out=statdump[:, sc * 8:(sc + 1) * 8], in_=statb[:, 0:8]),
                            inc=("sv", 1))
                c.wait("tensor", "sv")
                for dc in range(16):
                    pslot = ps1[:, 0:512] if (half * 16 + dc) % 2 == 0 else ps1[:, 512:1024]
                    c.wait("tensor", "sc_", lag=1)
                    for sc2 in range(4):
                        c.op("tensor", lambda t, dc=dc, sc2=sc2, pslot=pslot: t.transpose(
                            out=pslot[:, sc2 * 128:(sc2 + 1) * 128],
                            in_=xn[:, sc2 * D + dc * 128: sc2 * D + dc * 128 + 128],
                            identity=ident32[:]),
                            inc=("st_", 1) if sc2 == 3 else None)
                    c.wait("scalar", "st_")
                    c.wait("scalar", "sv", value=modv_ready)
                    c.op("scalar", lambda s, dc=dc, half=half, pslot=pslot: s.activation(
                        out=hT[:, dc * 1024 + half * 512: dc * 1024 + half * 512 + 512],
                        in_=pslot, func=AF.Identity, scale=ca1v[:, dc:dc + 1],
                        bias=sav[:, dc:dc + 1]), inc=("sc_", 1))
            if tap is not None:
                c.wait("sync", "sc_")
                c.op("sync", lambda s: s.dma_start(out=tap, in_=hT[:]), inc=("so", 16))
                if dbg:
                    c.wait("sync", "sv")
                    c.op("sync", lambda s: s.dma_start(out=DB["d_stat"], in_=statdump[:]),
                         inc=("so", 16))

        emit_LN(lambda sc: x_in[sc * 128:(sc + 1) * 128, :], ca1_c, sa_c,
                tap=DB["d_h1T"] if dbg else None)
        h1T_done = c.mark("sc_")
        if phases <= 1:
            c.wait("sync", "so")
            return

        wq4 = wqkv.rearrange("(dc p) (oc f) -> oc p dc f", p=P, f=128)
        gw4q = gw.rearrange("(dc p) (ic f) -> ic p dc f", p=P, f=128)
        uw4q = uw.rearrange("(dc p) (ic f) -> ic p dc f", p=P, f=128)
        dw4q = dw.rearrange("(ic p) o -> ic p o", p=P)

        pq_srcs = []
        for ici in range(48):
            pq_srcs += [(gw4q[ici], g8_sc, ici), (uw4q[ici], u8_sc, ici),
                        (dw4q[ici], d8_sc, ici)]

        PQ_LS = ["swa", "swb", "sw6", "sw7"]
        PQ_SS = ["sba", "sbb", "sg", "slb_"]

        def pq_load(k):
            sap, _, _ = pq_srcs[k]
            par = k % 4
            c.op("scalar", lambda s, sap=sap, par=par: s.dma_start(
                out=wst[par][:], in_=sap), inc=(PQ_LS[par], 16))

        def prequant_step(ic):
            if ic == 0:
                for k0 in range(4):
                    pq_load(k0)
            for j in range(3):
                k = 3 * ic + j
                par = k % 4
                _, dst, ici = pq_srcs[k]
                c.wait("gpsimd", PQ_LS[par])
                c.op("gpsimd", lambda g, dst=dst, ici=ici, par=par: g.dma_start(
                    out=dst.ap()[ici], in_=wst[par][:]), inc=(PQ_SS[par], 16))
                if k + 4 < len(pq_srcs):
                    c.wait("scalar", PQ_SS[par])
                    pq_load(k + 4)
        qk_sems = ["sea", "seb", "sla"]
        for oc in range(3):
            c.op("gpsimd", lambda g, oc=oc: g.dma_start(
                out=wqb3[oc % 3].rearrange("p (dc f) -> p dc f", f=128), in_=wq4[oc]),
                inc=(qk_sems[oc % 3], 16))
        c.wait("tensor", "sc_", value=h1T_done)
        svq = c.mark("sv")
        qkv_mm = {}
        regs = [ps1[:, 0:1024], ps1[:, 1024:2048], ps2[:, 0:1024], ps2[:, 1024:2048]]
        for oc in range(48):
            reg = regs[oc % 4]
            c.wait("tensor", qk_sems[oc % 3])
            if oc >= 4:
                c.wait("tensor", "sv", value=svq + (oc - 4) * 1 + 1)
            for dc in range(16):
                for sb2 in range(2):
                    c.op("tensor", lambda t, oc=oc, dc=dc, sb2=sb2, reg=reg: t.matmul(
                        out=reg[:, sb2 * 512:(sb2 + 1) * 512],
                        lhsT=wqb3[oc % 3][:, dc * 128:(dc + 1) * 128],
                        rhs=hT[:, dc * 1024 + sb2 * 512: dc * 1024 + sb2 * 512 + 512],
                        start=(dc == 0), stop=(dc == 15)),
                        inc=("st_", 1) if (dc == 15 and sb2 == 1) else None)
            qkv_mm[oc] = c.mark("st_")
            if oc + 3 < 48:
                c.wait("gpsimd", "st_", value=qkv_mm[oc])
                c.op("gpsimd", lambda g, oc=oc: g.dma_start(
                    out=wqb3[oc % 3].rearrange("p (dc f) -> p dc f", f=128), in_=wq4[oc + 3]),
                    inc=(qk_sems[oc % 3], 16))
            c.wait("vector", "st_")
            if oc >= 2:
                c.wait("vector", "sqa" if oc % 2 == 0 else "sqb")
            col = oc // 16
            c.op("vector", lambda v, oc=oc, reg=reg: v.tensor_copy(
                out=stage[oc % 2][:], in_=reg))
            c.chain("vector", lambda v, oc=oc: v.tensor_reduce(
                out=rmxt[:], in_=stage[oc % 2][:], axis=AX, op=A.max,
                apply_absolute_value=True))
            c.op("vector", lambda v, col=col: v.tensor_tensor(
                out=rmax[:, col:col + 1], in0=rmax[:, col:col + 1], in1=rmxt[:],
                op=A.max), inc=("sv", 1))
            c.wait("sync", "sv")
            c.op("sync", lambda s, oc=oc: s.dma_start(
                out=qkv_sc.ap()[oc * 128:(oc + 1) * 128, :], in_=stage[oc % 2][:]),
                inc=("sqa" if oc % 2 == 0 else "sqb", 16))
            prequant_step(oc)
        c.wait("sync", "sqa")
        c.wait("sync", "sqb")
        if dbg:
            c.op("sync", lambda s: s.dma_start(out=DB["d_qkv"], in_=qkv_sc.ap()),
                 inc=("so", 16))
        if phases <= 2:
            c.wait("sync", "so")
            return

        c.wait("sync", "sv")
        c.op("sync", lambda s: s.dma_start(out=mx_in.ap(), in_=rmax[:]), inc=("sx", 16))
        c.wait("gpsimd", "sx")
        c.op("gpsimd", lambda g: g.collective_compute(
            "AllReduce", A.max, replica_groups=RG,
            ins=[mx_in.ap().opt()], outs=[mx_out.ap().opt()]), inc=("scc", 1))
        c.wait("sync", "scc")
        for col in range(3):
            c.op("sync", lambda s, col=col: s.dma_start(
                out=rowbuf[0:1, col * 128:(col + 1) * 128],
                in_=mx_out.ap().transpose([1, 0])[col:col + 1, :],
                allow_slow_non_contiguous=True), inc=("sx", 16))
        c.wait("vector", "sx")
        for col in range(3):
            c.chain("vector", lambda v, col=col: v.reduce_max(
                out=rowb2[0:1, col:col + 1], in_=rowbuf[0:1, col * 128:(col + 1) * 128],
                axis=AX))
        CR = 16
        c.chain("vector", lambda v: v.reciprocal(out=rowb2[0:1, 4:7], in_=rowb2[0:1, 0:3]))
        c.op("vector", lambda v: v.tensor_scalar(
            out=rowb2[0:1, CR + HIQ:CR + HIQ + 3], in0=rowb2[0:1, 4:7], scalar1=224.0,
            scalar2=None, op0=A.mult))
        c.op("vector", lambda v: v.tensor_scalar(
            out=rowb2[0:1, CR + I5Q:CR + I5Q + 3], in0=rowb2[0:1, 4:7], scalar1=229376.0,
            scalar2=None, op0=A.mult))
        c.chain("vector", lambda v: v.tensor_scalar(
            out=rowb2[0:1, 8:11], in0=rowb2[0:1, 0:3], scalar1=1.0 / 14336.0,
            scalar2=None, op0=A.mult))
        c.op("vector", lambda v: v.tensor_tensor(
            out=rowb2[0:1, CR + THQ:CR + THQ + 3], in0=rowb2[0:1, 8:11],
            in1=rowb2[0:1, 8:11], op=A.mult))
        c.chain("vector", lambda v: v.tensor_tensor(
            out=rowb2[0:1, 11:12], in0=rowb2[0:1, 0:1], in1=rowb2[0:1, 1:2], op=A.mult))
        c.chain("vector", lambda v: v.tensor_scalar(
            out=rowb2[0:1, CR + CEXP:CR + CEXP + 1], in0=rowb2[0:1, 11:12],
            scalar1=1.0 / (448.0 * 448.0 * SQ128), scalar2=None, op0=A.mult))
        c.op("vector", lambda v: v.tensor_scalar(
            out=rowb2[0:1, CR + VSC:CR + VSC + 1], in0=rowb2[0:1, 2:3],
            scalar1=1.0 / 448.0, scalar2=None, op0=A.mult))
        c.op("vector", lambda v: v.tensor_scalar(
            out=rowb2[0:1, CR + NEGC:CR + NEGC + 1],
            in0=rowb2[0:1, CR + CEXP:CR + CEXP + 1],
            scalar1=-1.0, scalar2=None, op0=A.mult), inc=("sv", 1))
        c.wait("tensor", "sv")
        c.op("tensor", lambda t: t.matmul(
            out=ps1[:, 0:16], lhsT=ones1[:], rhs=rowb2[0:1, CR:CR + 16],
            start=True, stop=True), inc=("st_", 1))
        c.wait("vector", "st_")
        c.op("vector", lambda v: v.tensor_copy(out=constsA[:], in_=ps1[:, 0:16]),
             inc=("sv", 1))
        if dbg:
            c.wait("sync", "sv")
            c.op("sync", lambda s: s.dma_start(out=DB["d_mx"], in_=rowb2[0:1, 0:32]),
                 inc=("so", 16))
            c.op("sync", lambda s: s.dma_start(out=DB["d_rmax"].tensor.ap()[:, 0:4], in_=rmax[:]),
                 inc=("so", 16))
            c.op("sync", lambda s: s.dma_start(out=DB["d_rmax"].tensor.ap()[:, 4:8], in_=mx_out.ap()),
                 inc=("so", 16))

        consts_ready_v = c.mark("sv")
        sxq2 = c.mark("sx")
        nrm_ps = [ps1[0:1, 0:512], ps1[0:1, 512:1024]]

        ROLE_ORDER = [1, 2, 0]
        for t_i in range(48):
            role = ROLE_ORDER[t_i // 16]
            oc = t_i % 16
            src_row = role * 16 + oc
            stg = stage[t_i % 2]
            if t_i >= 2:
                pass
            if t_i == 0:
                c.wait("sync", "sqa")
                c.wait("sync", "sqb")
            c.wait("sync", "sfa" if t_i % 2 == 0 else "sfb")
            c.op("sync", lambda s, src_row=src_row, stg=stg: s.dma_start(
                out=stg[:], in_=qkv_sc.ap()[src_row * 128:(src_row + 1) * 128, :]),
                inc=("sx", 16))
            c.wait("scalar", "sx")
            c.wait("scalar", "sv", value=consts_ready_v)
            c.op("scalar", lambda s, stg=stg, role=role: s.activation(
                out=fp8buf[:], in_=stg, func=AF.Copy, scale=constsA[:, HIQ + role:HIQ + role + 1]),
                inc=("sfa" if t_i % 2 == 0 else "sfb", 1))
            c.op("scalar", lambda s: s.activation(
                out=mainb[:], in_=fp8buf[:], func=AF.Copy, scale=cimm[:, 1:2]),
                inc=("sc_", 1))
            c.wait("vector", "sx")
            c.wait("vector", "sv", value=consts_ready_v)
            c.op("vector", lambda v, stg=stg, role=role: v.tensor_scalar(
                out=zb[:], in0=stg, scalar1=constsA[:, I5Q + role:I5Q + role + 1],
                scalar2=C_RNE, op0=A.mult, op1=A.add))
            c.op("vector", lambda v: v.tensor_scalar(
                out=fixb[:], in0=zb[:], scalar1=C_RNE, scalar2=1.0 / 512.0,
                op0=A.subtract, op1=A.mult))
            c.op("vector", lambda v, stg=stg: v.tensor_tensor(
                out=sqb[:], in0=stg, in1=stg, op=A.mult),
                inc=("sfa" if t_i % 2 == 0 else "sfb", 1))
            c.op("vector", lambda v, role=role: v.tensor_scalar(
                out=mskb[:], in0=sqb[:], scalar1=constsA[:, THQ + role:THQ + role + 1],
                scalar2=None, op0=A.is_lt))
            c.wait("vector", "sc_")
            dest = (q8T[:, oc * 1024:(oc + 1) * 1024] if role == 0
                    else (k8stage[:] if role == 1 else v8q[:]))
            c.op("vector", lambda v, dest=dest: v.select(
                out=dest, mask=mskb[:], on_true=fixb[:], on_false=mainb[:]),
                inc=("sv", 1))
            if role == 0:
                c.wait("scalar", "sv")
                c.op("scalar", lambda s, oc=oc: s.activation(
                    out=sqnb[:], in_=q8T[:, oc * 1024:(oc + 1) * 1024], func=AF.Square),
                    inc=("sc_", 1))
                c.wait("tensor", "sc_")
                c.wait("tensor", "sv", lag=0)
                for hf in range(2):
                    c.op("tensor", lambda t, hf=hf: t.matmul(
                        out=nrm_ps[hf], lhsT=onescol[:],
                        rhs=sqnb[:, hf * 512:(hf + 1) * 512], start=True, stop=True),
                        inc=("st_", 1) if hf == 1 else None)
                c.wait("vector", "st_")
                c.op("vector", lambda v: v.reduce_max(
                    out=rowb2[0:1, 40:41], in_=ps1[0:1, 0:512], axis=AX))
                c.chain("vector", lambda v: v.reduce_max(
                    out=rowb2[0:1, 41:42], in_=ps1[0:1, 512:1024], axis=AX))
                c.op("vector", lambda v, oc=oc: v.tensor_tensor(
                    out=rowb2[0:1, 42 + oc:43 + oc], in0=rowb2[0:1, 40:41],
                    in1=rowb2[0:1, 41:42], op=A.max), inc=("sv", 1))
            elif role == 1:
                c.wait("sync", "sv")
                c.op("sync", lambda s, oc=oc: s.dma_start(
                    out=k8_in.ap()[oc * 128:(oc + 1) * 128, :], in_=k8stage[:]),
                    inc=("sx", 16))
                c.wait("scalar", "sv")
                c.op("scalar", lambda s: s.activation(
                    out=sqnb[:], in_=k8stage[:], func=AF.Square), inc=("sc_", 1))
                c.wait("tensor", "sc_")
                for hf in range(2):
                    c.op("tensor", lambda t, hf=hf: t.matmul(
                        out=nrm_ps[hf], lhsT=onescol[:],
                        rhs=sqnb[:, hf * 512:(hf + 1) * 512], start=True, stop=True),
                        inc=("st_", 1) if hf == 1 else None)
                c.wait("vector", "st_")
                c.op("vector", lambda v: v.reduce_max(
                    out=rowb2[0:1, 40:41], in_=ps1[0:1, 0:512], axis=AX))
                c.chain("vector", lambda v: v.reduce_max(
                    out=rowb2[0:1, 41:42], in_=ps1[0:1, 512:1024], axis=AX))
                c.op("vector", lambda v, oc=oc: v.tensor_tensor(
                    out=rowb2[0:1, 58 + oc:59 + oc], in0=rowb2[0:1, 40:41],
                    in1=rowb2[0:1, 41:42], op=A.max), inc=("sv", 1))
                if oc == 15:
                    c.wait("sync", "sx")
                    c.op("sync", lambda s: s.dma_start(
                        out=kn_in.ap(), in_=rowb2[0:1, 58:74]), inc=("sx", 16))
                    c.wait("gpsimd", "sx")
                    c.op("gpsimd", lambda g: g.collective_compute(
                        "AllGather", A.bypass, replica_groups=RG,
                        ins=[k8_in.ap().opt()], outs=[k8_out.ap().opt()]), inc=("scc", 1))
                    c.op("gpsimd", lambda g: g.collective_compute(
                        "AllReduce", A.max, replica_groups=RG,
                        ins=[kn_in.ap().opt()], outs=[kn_out.ap().opt()]), inc=("scc", 1))
            else:
                c.wait("tensor", "sv")
                c.wait("tensor", "sc_", lag=0)
                for j in range(8):
                    c.op("tensor", lambda t, j=j: t.transpose(
                        out=psT[:, j * 128:(j + 1) * 128],
                        in_=v8q[:, j * 128:(j + 1) * 128], identity=ident16[:]),
                        inc=("st_", 1) if j == 7 else None)
                c.wait("vector", "st_")
                c.wait("vector", "sx")
                c.op("vector", lambda v: v.tensor_copy(out=v8stage[:], in_=psT[:]),
                     inc=("sv", 1))
                c.wait("sync", "sv")
                c.op("sync", lambda s, oc=oc: s.dma_start(
                    out=v8_in.ap()[oc * 1024:(oc + 1) * 1024, :].rearrange(
                        "(j p) f -> p j f", p=P),
                    in_=v8stage.rearrange("p (j f) -> p j f", f=128)), inc=("sx", 16))
        c.wait("gpsimd", "sx")
        c.op("gpsimd", lambda g: g.collective_compute(
            "AllGather", A.bypass, replica_groups=RG,
            ins=[v8_in.ap().opt()], outs=[v8_out.ap().opt()]), inc=("scc", 1))
        AG_done = c.mark("scc")

        c.wait("sync", "scc", value=AG_done - 1)
        c.op("sync", lambda s: s.dma_start(out=rowb2[0:1, 74:90], in_=kn_out.ap()),
             inc=("sx", 16))
        c.wait("vector", "sx")
        c.op("vector", lambda v: v.tensor_tensor(
            out=rowb2[0:1, 90:106], in0=rowb2[0:1, 42:58], in1=rowb2[0:1, 74:90],
            op=A.mult), inc=("sv", 1))
        c.wait("vector", "sv")
        c.wait("scalar", "sv")
        c.op("scalar", lambda s: s.activation(
            out=rowb2[0:1, 106:122], in_=rowb2[0:1, 90:106], func=AF.Sqrt),
            inc=("sc_", 1))
        c.wait("vector", "sc_")
        c.op("vector", lambda v: v.tensor_scalar(
            out=rowb2[0:1, 122:138], in0=rowb2[0:1, 106:122],
            scalar1=rowb2[0:1, CR + NEGC:CR + NEGC + 1], scalar2=None, op0=A.mult),
            inc=("sv", 1))
        c.wait("tensor", "sv")
        c.op("tensor", lambda t: t.matmul(
            out=ps1[:, 16:32], lhsT=ones1[:], rhs=rowb2[0:1, 122:138],
            start=True, stop=True), inc=("st_", 1))
        c.wait("vector", "st_")
        c.op("vector", lambda v: v.tensor_copy(out=negcB[:], in_=ps1[:, 16:32]),
             inc=("sv", 1))
        attn_consts_v = c.mark("sv")
        if dbg:
            c.wait("sync", "sv")
            c.op("sync", lambda s: s.dma_start(out=DB["d_negc"], in_=rowb2[0:1, 74:138]),
                 inc=("so", 16))
            c.op("sync", lambda s: s.dma_start(out=DB["d_q8T"], in_=q8T[:]), inc=("so", 16))
            c.wait("sync", "scc", value=AG_done)
            c.op("sync", lambda s: s.dma_start(out=DB["d_k8o"], in_=k8_out.ap()), inc=("so", 16))
            c.op("sync", lambda s: s.dma_start(out=DB["d_v8o"], in_=v8_out.ap()), inc=("so", 16))
            c.op("sync", lambda s: s.dma_start(out=DB["d_v8i"], in_=v8_in.ap()), inc=("so", 16))

        if phases <= 4:
            c.wait("sync", "so")
            return
        c.wait("gpsimd", "sv")
        for b in range(2):
            c.op("gpsimd", lambda g, b=b: g.memset(
                v8h[b].rearrange("p (kc e) -> p kc e", e=129)[:, :, 128:129], 1.0),
                inc=("sq", 1))
        def head_loads(h):
            buf = h % 2
            hs = "sla" if buf == 0 else "slb_"
            for r in range(NC8):
                c.op("scalar", lambda s, h=h, r=r, buf=buf: s.dma_start(
                    out=k8h[buf][:, r * 1024:(r + 1) * 1024],
                    in_=k8_out.ap()[r * D + h * 128: r * D + (h + 1) * 128, :]),
                    inc=(hs, 16))
            for r in range(NC8):
                c.op("scalar", lambda s, h=h, r=r, buf=buf: s.dma_start(
                    out=v8h[buf].rearrange("p (kc e) -> p kc e", e=129)[:, r * 8:(r + 1) * 8, 0:128],
                    in_=v8_out.ap()[r * NH * SC + h * SC: r * NH * SC + (h + 1) * SC, :]
                    .rearrange("(j p) f -> p j f", p=P)), inc=(hs, 16))

        c.wait("scalar", "scc", value=AG_done)
        c.wait("scalar", "sq")
        c.wait("scalar", "sc_")
        head_loads(0)
        head_loads(1)
        psS = [ps1[:, 0:1024], ps2[:, 0:1024], ps1[:, 1024:2048]]
        pvt = [ps2[:, 1024 + 256 * qs: 1024 + 256 * qs + 129] for qs in range(4)]
        for h in range(NH):
            c.wait("tensor", "sla" if h % 2 == 0 else "slb_")
            for Qb in range(2):
                c.wait("tensor", "sv")
                sc_done = {}
                ex_done = {}
                pv_done = {}

                def scores_g(g):
                    par = g % 3
                    if g - 3 >= 0:
                        c.wait("tensor", "sc_", value=ex_done[g - 3])
                    for kcl in range(2):
                        kc = 2 * g + kcl
                        c.op("tensor", lambda t, h=h, Qb=Qb, kc=kc, kcl=kcl, par=par: t.matmul(
                            out=psS[par][:, kcl * 512:(kcl + 1) * 512],
                            lhsT=k8h[h % 2][:, kc * 128:(kc + 1) * 128],
                            rhs=q8T[:, h * 1024 + Qb * 512: h * 1024 + Qb * 512 + 512],
                            start=True, stop=True),
                            inc=("st_", 1) if kcl == 1 else None)
                    sc_done[g] = c.mark("st_")

                scores_g(0)
                scores_g(1)
                for g in range(32):
                    par = g % 3
                    wv = sc_done[g]
                    if g - 2 >= 0:
                        wv = max(wv, pv_done[g - 2])
                    c.wait("scalar", "st_", value=wv)
                    c.wait("scalar", "sv", value=attn_consts_v)
                    c.op("scalar", lambda s, h=h, par=par: s.activation(
                        out=PT[g % 2][:], in_=psS[par], func=AF.Exp,
                        scale=constsA[:, CEXP:CEXP + 1], bias=negcB[:, h:h + 1]),
                        inc=("sc_", 1))
                    ex_done[g] = c.mark("sc_")
                    if g + 2 < 32:
                        scores_g(g + 2)
                    c.wait("tensor", "sc_", value=ex_done[g])
                    for kcl in range(2):
                        for qs in range(4):
                            first = (g == 0 and kcl == 0)
                            c.op("tensor", lambda t, kcl=kcl, qs=qs, first=first, g=g: t.matmul(
                                out=pvt[qs],
                                lhsT=PT[g % 2][:, kcl * 512 + qs * 128: kcl * 512 + qs * 128 + 128],
                                rhs=v8h[h % 2][:, (2 * g + kcl) * 129: (2 * g + kcl) * 129 + 129],
                                start=(first and qs % 2 == 0), stop=(g == 31 and kcl == 1),
                                skip_group_check=True),
                                inc=("st_", 1) if (kcl == 1 and qs == 3) else None)
                    pv_done[g] = c.mark("st_")
                c.wait("vector", "st_")
                for qs in range(3):
                    c.op("vector", lambda v, qs=qs: v.reciprocal(
                        out=rl4[:, qs:qs + 1], in_=pvt[qs][:, 128:129]))
                c.chain("vector", lambda v: v.reciprocal(
                    out=rl4[:, 3:4], in_=pvt[3][:, 128:129]))
                for qs in range(4):
                    c.op("vector", lambda v, qs=qs: v.tensor_scalar(
                        out=attnstage[:, qs * 128:(qs + 1) * 128], in0=pvt[qs][:, 0:128],
                        scalar1=rl4[:, qs:qs + 1], scalar2=constsA[:, VSC:VSC + 1],
                        op0=A.mult, op1=A.mult), inc=("sv", 1) if qs == 3 else None)
                c.wait("tensor", "sv")
                for qs in range(4):
                    c.op("tensor", lambda t, qs=qs: t.transpose(
                        out=psT[:, qs * 128:(qs + 1) * 128],
                        in_=attnstage[:, qs * 128:(qs + 1) * 128], identity=ident16[:]),
                        inc=("st_", 1) if qs == 3 else None)
                c.wait("vector", "st_")
                c.op("vector", lambda v, h=h, Qb=Qb: v.tensor_copy(
                    out=aoT[:, h * 1024 + Qb * 512: h * 1024 + Qb * 512 + 512],
                    in_=psT[:, 0:512]), inc=("sv", 1))
            if h + 2 < NH:
                c.wait("scalar", "st_")
                head_loads(h + 2)
        if dbg:
            c.wait("sync", "sv")
            c.op("sync", lambda s: s.dma_start(out=DB["d_aoT"], in_=aoT[:]), inc=("so", 16))

        if phases <= 5:
            c.wait("sync", "so")
            return
        c.wait("sync", "sv")
        c.op("sync", lambda s: s.dma_start(
            out=rowbuf[0:1, 0:D], in_=modflat[0:1, 2 * D:3 * D]), inc=("sx", 16))
        c.wait("tensor", "sx")
        for cb in range(4):
            c.op("tensor", lambda t, cb=cb: t.matmul(
                out=ps1[:, cb * 512:(cb + 1) * 512], lhsT=ones1[:],
                rhs=rowbuf[0:1, cb * 512:(cb + 1) * 512], start=True, stop=True),
                inc=("st_", 1) if cb == 3 else None)
        c.wait("vector", "st_")
        c.op("vector", lambda v: v.tensor_copy(out=gaB[:], in_=ps1[:, 0:2048]),
             inc=("sv", 1))
        c.wait("sync", "st_")
        c.op("sync", lambda s: s.dma_start(out=rowbuf[0:1, 0:D], in_=bout), inc=("sx", 16))
        c.wait("tensor", "sx")
        c.wait("tensor", "sv")
        for cb in range(4):
            c.op("tensor", lambda t, cb=cb: t.matmul(
                out=ps1[:, cb * 512:(cb + 1) * 512], lhsT=ones1[:],
                rhs=rowbuf[0:1, cb * 512:(cb + 1) * 512], start=True, stop=True),
                inc=("st_", 1) if cb == 3 else None)
        c.op("vector", lambda v: v.tensor_scalar(
            out=opscr[:], in0=gaB[:], scalar1=0.0, scalar2=None, op0=A.is_ge))
        c.op("vector", lambda v: v.tensor_scalar(
            out=opscr[:], in0=opscr[:], scalar1=2.0, scalar2=1.0, op0=A.mult,
            op1=A.subtract))
        c.op("vector", lambda v: v.tensor_scalar(
            out=gabB[:], in0=gaB[:], scalar1=-1.0, scalar2=None, op0=A.mult))
        c.op("vector", lambda v: v.tensor_tensor(
            out=gaB[:], in0=gaB[:], in1=gabB[:], op=A.max))
        c.op("vector", lambda v: v.tensor_scalar(
            out=gaB[:], in0=gaB[:], scalar1=0.25, scalar2=None, op0=A.max))
        c.op("vector", lambda v: v.tensor_tensor(
            out=gaB[:], in0=gaB[:], in1=opscr[:], op=A.mult), inc=("sv", 1))
        c.wait("vector", "st_")
        c.op("vector", lambda v: v.tensor_tensor(
            out=gabB[:], in0=gaB[:], in1=ps1[:, 0:2048], op=A.mult), inc=("sv", 1))
        gab_ready = c.mark("sv")

        wo4 = wout.rearrange("(hc p) o -> hc p o", p=P)
        units = [(ob, sh) for ob in range(4) for sh in range(2)]
        banks = [ps1[:, i * 512:(i + 1) * 512] for i in range(4)] + \
                [ps2[:, i * 512:(i + 1) * 512] for i in range(4)]
        c.wait("tensor", "sv")
        c.wait("gpsimd", "st_")
        c.wait("sync", "st_")
        svo = c.mark("sv")
        wo_ld = {}
        wo_mm = {}
        for i, (ob, sh) in enumerate(units):
            par = i % 2
            bgrp = banks[0:4] if par == 0 else banks[4:8]
            for hc in range(8):
                k = i * 16 + hc
                if k - 8 in wo_mm:
                    c.wait("gpsimd", "st_", value=wo_mm[k - 8])
                c.op("gpsimd", lambda g, ob=ob, hc=hc: g.dma_start(
                    out=woutb8[hc % 8][:],
                    in_=wo4[hc][:, ob * 512:(ob + 1) * 512]), inc=(f"sw{hc % 8}", 16))
            if i >= 2:
                c.wait("tensor", "sv", value=svo + (i - 1) * 4)
            for hc in range(NH):
                k = i * 16 + hc
                c.wait("tensor", f"sw{hc % 8}")
                for sc2 in range(4):
                    sc = sh * 4 + sc2
                    c.op("tensor", lambda t, hc=hc, sc=sc, sc2=sc2, bgrp=bgrp: t.matmul(
                        out=bgrp[sc2],
                        lhsT=aoT[:, hc * 1024 + sc * 128: hc * 1024 + sc * 128 + 128],
                        rhs=woutb8[hc % 8][:],
                        start=(hc == 0), stop=(hc == 15)),
                        inc=("st_", 1) if sc2 == 3 else None)
                wo_mm[k] = c.mark("st_")
                if hc + 8 < NH:
                    c.wait("gpsimd", "st_", value=wo_mm[k])
                    c.op("gpsimd", lambda g, ob=ob, hc=hc: g.dma_start(
                        out=woutb8[(hc + 8) % 8][:],
                        in_=wo4[hc + 8][:, ob * 512:(ob + 1) * 512]),
                        inc=(f"sw{(hc + 8) % 8}", 16))
            for sc2 in range(4):
                sc = sh * 4 + sc2
                c.op("sync", lambda s, sc=sc, ob=ob, sc2=sc2: s.dma_start(
                    out=xsl[sc2 % 2][:],
                    in_=x_in[sc * 128:(sc + 1) * 128, ob * 512:(ob + 1) * 512]),
                    inc=("sx", 16))
                c.wait("vector", "st_")
                c.wait("vector", "sx", lag=0)
                c.wait("vector", "sv", value=gab_ready)
                c.op("vector", lambda v, sc2=sc2, ob=ob, bgrp=bgrp: v.tensor_tensor(
                    out=epis[:], in0=bgrp[sc2], in1=gaB[:, ob * 512:(ob + 1) * 512],
                    op=A.mult))
                c.op("vector", lambda v, ob=ob: v.tensor_tensor(
                    out=epis[:], in0=epis[:], in1=gabB[:, ob * 512:(ob + 1) * 512],
                    op=A.add))
                c.op("vector", lambda v, sc2=sc2: v.tensor_tensor(
                    out=x2b[sc2 % 2][:], in0=epis[:], in1=xsl[sc2 % 2][:], op=A.add),
                    inc=("sv", 1))
                c.wait("sync", "sv")
                c.op("sync", lambda s, sc=sc, ob=ob, sc2=sc2: s.dma_start(
                    out=x2_sc.ap()[sc * 128:(sc + 1) * 128, ob * 512:(ob + 1) * 512],
                    in_=x2b[sc2 % 2][:]), inc=("sx", 16))
        x2_done_sx = c.mark("sx")
        if dbg:
            c.wait("sync", "sx", value=x2_done_sx)
            c.op("sync", lambda s: s.dma_start(out=DB["d_x2"], in_=x2_sc.ap()),
                 inc=("so", 16))

        c.wait("sync", "sx", value=x2_done_sx)
        emit_LN(lambda sc: x2_sc.ap()[sc * 128:(sc + 1) * 128, :], cb1_c, sb_c,
                tap=DB["d_h2T"] if dbg else None)
        h2T_done = c.mark("sc_")

        if phases <= 6:
            c.wait("sync", "so")
            return
        gw4 = gw.rearrange("(dc p) (ic f) -> ic p dc f", p=P, f=128)
        uw4 = uw.rearrange("(dc p) (ic f) -> ic p dc f", p=P, f=128)
        c.wait("gpsimd", "sx", value=x2_done_sx)
        c.wait("gpsimd", "sba")
        c.wait("gpsimd", "sbb")
        c.wait("gpsimd", "sg")
        c.wait("gpsimd", "slb_")
        c.wait("gpsimd", "sc_", value=h2T_done)
        gu_sems = ["sea", "seb", "sla"]
        for ic in range(3):
            sj = gu_sems[ic % 3]
            c.op("gpsimd", lambda g, ic=ic: g.dma_start(
                out=gwb3[ic % 3][:], in_=g8_sc.ap()[ic]), inc=(sj, 16))
            c.op("gpsimd", lambda g, ic=ic: g.dma_start(
                out=uwb3[ic % 3][:], in_=u8_sc.ap()[ic]), inc=(sj, 16))
        c.wait("tensor", "sc_", value=h2T_done)
        svf = c.mark("sv")
        ffn_mm = {}
        for ic in range(48):
            reg = ps1 if ic % 2 == 0 else ps2
            c.wait("tensor", gu_sems[ic % 3])
            if ic >= 2:
                c.wait("tensor", "sv", value=svf + (ic - 2) + 1)
            for dc in range(16):
                for sb2 in range(2):
                    c.op("tensor", lambda t, ic=ic, dc=dc, sb2=sb2, reg=reg: t.matmul(
                        out=reg[:, sb2 * 512:(sb2 + 1) * 512],
                        lhsT=gwb3[ic % 3][:, dc * 128:(dc + 1) * 128],
                        rhs=hT[:, dc * 1024 + sb2 * 512: dc * 1024 + sb2 * 512 + 512],
                        start=(dc == 0), stop=(dc == 15)))
                for sb2 in range(2):
                    c.op("tensor", lambda t, ic=ic, dc=dc, sb2=sb2, reg=reg: t.matmul(
                        out=reg[:, 1024 + sb2 * 512: 1024 + (sb2 + 1) * 512],
                        lhsT=uwb3[ic % 3][:, dc * 128:(dc + 1) * 128],
                        rhs=hT[:, dc * 1024 + sb2 * 512: dc * 1024 + sb2 * 512 + 512],
                        start=(dc == 0), stop=(dc == 15)),
                        inc=("st_", 1) if (dc == 15 and sb2 == 1) else None)
            ffn_mm[ic] = c.mark("st_")
            if ic + 3 < 48:
                c.wait("gpsimd", "st_", value=ffn_mm[ic])
                sj = gu_sems[ic % 3]
                c.op("gpsimd", lambda g, ic=ic: g.dma_start(
                    out=gwb3[ic % 3][:], in_=g8_sc.ap()[ic + 3]), inc=(sj, 16))
                c.op("gpsimd", lambda g, ic=ic: g.dma_start(
                    out=uwb3[ic % 3][:], in_=u8_sc.ap()[ic + 3]), inc=(sj, 16))
            c.wait("scalar", "st_")
            c.wait("scalar", "sv", lag=0)
            c.op("scalar", lambda s, ic=ic, reg=reg: s.activation(
                out=sgbuf[:], in_=reg[:, 0:1024], func=AF.Silu,
                scale=gscol[:, ic:ic + 1]), inc=("sc_", 1))
            c.wait("vector", "sc_")
            c.op("vector", lambda v, ic=ic, reg=reg: v.scalar_tensor_tensor(
                out=fuseT[:, ic * 1024:(ic + 1) * 1024], in0=reg[:, 1024:2048],
                scalar=uscol[:, ic:ic + 1], in1=sgbuf[:], op0=A.mult, op1=A.mult),
                inc=("sv", 1))
        fuse_done = c.mark("sv")

        c.op("sync", lambda s: s.dma_start(
            out=rowbuf[0:1, 0:D], in_=modflat[0:1, 5 * D:6 * D]), inc=("sx", 16))
        c.wait("tensor", "sx")
        c.wait("tensor", "sv", value=fuse_done)
        for cb in range(4):
            c.op("tensor", lambda t, cb=cb: t.matmul(
                out=ps1[:, cb * 512:(cb + 1) * 512], lhsT=ones1[:],
                rhs=rowbuf[0:1, cb * 512:(cb + 1) * 512], start=True, stop=True),
                inc=("st_", 1) if cb == 3 else None)
        c.wait("vector", "st_")
        c.op("vector", lambda v: v.tensor_copy(out=dsb2B[:], in_=ps1[:, 0:2048]),
             inc=("sv", 1))
        c.wait("sync", "st_")
        c.op("sync", lambda s: s.dma_start(out=rowbuf[0:1, 0:D], in_=dsv), inc=("sx", 16))
        c.wait("tensor", "sx")
        c.wait("tensor", "sv")
        for cb in range(4):
            c.op("tensor", lambda t, cb=cb: t.matmul(
                out=ps1[:, cb * 512:(cb + 1) * 512], lhsT=ones1[:],
                rhs=rowbuf[0:1, cb * 512:(cb + 1) * 512], start=True, stop=True),
                inc=("st_", 1) if cb == 3 else None)
        c.op("vector", lambda v: v.tensor_scalar(
            out=dscr[:], in0=dsb2B[:], scalar1=0.0, scalar2=None, op0=A.is_ge))
        c.op("vector", lambda v: v.tensor_scalar(
            out=dscr[:], in0=dscr[:], scalar1=2.0, scalar2=1.0, op0=A.mult,
            op1=A.subtract))
        c.op("vector", lambda v: v.tensor_scalar(
            out=fx2all[:, 0:2048], in0=dsb2B[:], scalar1=-1.0, scalar2=None, op0=A.mult))
        c.op("vector", lambda v: v.tensor_tensor(
            out=dsb2B[:], in0=dsb2B[:], in1=fx2all[:, 0:2048], op=A.max))
        c.op("vector", lambda v: v.tensor_scalar(
            out=dsb2B[:], in0=dsb2B[:], scalar1=0.25, scalar2=None, op0=A.max))
        c.op("vector", lambda v: v.tensor_tensor(
            out=dsb2B[:], in0=dsb2B[:], in1=dscr[:], op=A.mult))
        c.wait("vector", "st_")
        c.op("vector", lambda v: v.tensor_tensor(
            out=dsb2B[:], in0=dsb2B[:], in1=ps1[:, 0:2048], op=A.mult), inc=("sv", 1))
        dsb2_ready = c.mark("sv")

        dw4 = dw.rearrange("(ic p) o -> ic p o", p=P)
        dw_sems = ["sw0", "sw1", "sw2", "sw3", "sw4", "sw5"]
        dmm = {}
        for m in range(6):
            c.op("gpsimd", lambda g, m=m: g.dma_start(
                out=dwb6[m % 6][:], in_=d8_sc.ap()[m % 48][:, (m // 48) * 512:(m // 48) * 512 + 512]),
                inc=(dw_sems[m % 6], 16))
        c.wait("tensor", "sv", value=dsb2_ready)
        sob = c.mark("so")
        for ob in range(4):
            c.wait("sync", "st_")
            c.wait("sync", "sv")
            sxd = c.mark("sx")
            for sc in range(8):
                c.op("sync", lambda s, sc=sc, ob=ob: s.dma_start(
                    out=fx2all[:, sc * 512:(sc + 1) * 512],
                    in_=x2_sc.ap()[sc * 128:(sc + 1) * 128, ob * 512:(ob + 1) * 512]),
                    inc=("sx", 16))
            if ob > 0:
                c.wait("tensor", "sv")
            for ic in range(48):
                m = ob * 48 + ic
                c.wait("tensor", dw_sems[m % 6])
                for sc in range(8):
                    c.op("tensor", lambda t, ic=ic, sc=sc, m=m: t.matmul(
                        out=banks[sc],
                        lhsT=fuseT[:, ic * 1024 + sc * 128: ic * 1024 + sc * 128 + 128],
                        rhs=dwb6[m % 6][:],
                        start=(ic == 0), stop=(ic == 47)),
                        inc=("st_", 1) if sc == 7 else None)
                dmm[m] = c.mark("st_")
                if m + 6 < 192:
                    m2 = m + 6
                    c.wait("gpsimd", "st_", value=dmm[m])
                    c.op("gpsimd", lambda g, m2=m2: g.dma_start(
                        out=dwb6[m2 % 6][:],
                        in_=d8_sc.ap()[m2 % 48][:, (m2 // 48) * 512:(m2 // 48) * 512 + 512]),
                        inc=(dw_sems[m2 % 6], 16))
            c.wait("vector", "st_")
            c.wait("vector", "sx", value=sxd + 16 * 8)
            c.wait("vector", "sv", value=dsb2_ready)
            for sc in range(8):
                n_st = ob * 8 + sc
                if n_st >= 2:
                    c.wait("vector", "so", value=sob + 16 * (n_st - 1))
                c.op("vector", lambda v, sc=sc, ob=ob: v.tensor_tensor(
                    out=fepi[sc % 2][:], in0=banks[sc],
                    in1=dsb2B[:, ob * 512:(ob + 1) * 512], op=A.mult))
                c.op("vector", lambda v, sc=sc: v.tensor_tensor(
                    out=fepi[sc % 2][:], in0=fepi[sc % 2][:],
                    in1=fx2all[:, sc * 512:(sc + 1) * 512], op=A.add), inc=("sv", 1))
                c.wait("sync", "sv")
                c.op("sync", lambda s, sc=sc, ob=ob: s.dma_start(
                    out=y_out[sc * 128:(sc + 1) * 128, ob * 512:(ob + 1) * 512],
                    in_=fepi[sc % 2][:]), inc=("so", 16))
        if dbg:
            c.op("sync", lambda s: s.dma_start(out=DB["d_v8late"], in_=v8_out.ap()),
                 inc=("so", 16))
        c.wait("sync", "so")

    with nc.Block() as block:
        @block.sync
        def _(eng):
            script(Ctx("sync", eng, sems))

        @block.scalar
        def _(eng):
            script(Ctx("scalar", eng, sems))

        @block.gpsimd
        def _(eng):
            with allow_ici_all_to_all():
                script(Ctx("gpsimd", eng, sems))

        @block.vector
        def _(eng):
            script(Ctx("vector", eng, sems))

        @block.tensor
        def _(eng):
            script(Ctx("tensor", eng, sems))

    stack.close()
    return nc


def make_in_maps(inputs):
    x = np.ascontiguousarray(inputs["L0_attn_x"].reshape(NC8 * SC, D))
    wqkv = np.ascontiguousarray(inputs["L0_attn_W_qkv"])
    wout = np.ascontiguousarray(inputs["L0_attn_W_out"])
    bout = np.ascontiguousarray(inputs["L0_attn_b_out"]).reshape(1, D)
    gw = np.ascontiguousarray(inputs["L0_ffn_gate_w"])
    uw = np.ascontiguousarray(inputs["L0_ffn_up_w"])
    dw = np.ascontiguousarray(inputs["L0_ffn_down_w"])
    gsv = np.ascontiguousarray(inputs["L0_ffn_gate_scale"]).reshape(1, FF)
    usv = np.ascontiguousarray(inputs["L0_ffn_up_scale"]).reshape(1, FF)
    dsv = np.ascontiguousarray(inputs["L0_ffn_down_scale"]).reshape(1, D)
    cvec = np.ascontiguousarray(inputs["L0_c"]).reshape(1, D)
    wmod = np.ascontiguousarray(inputs["L0_wmod"])
    maps = []
    for r in range(NC8):
        maps.append({
            "x": np.ascontiguousarray(x[r * SC:(r + 1) * SC]),
            "wqkv": wqkv, "wout": wout, "bout": bout,
            "gw": gw, "uw": uw, "dw": dw, "gsv": gsv, "usv": usv, "dsv": dsv,
            "cvec": cvec,
            "wmod": np.ascontiguousarray(wmod[:, r * 1536:(r + 1) * 1536]),
        })
    return maps


def kernel(**inputs):
    nc = build_nc(dbg=False)
    maps = make_in_maps(inputs)
    res = run_bass_kernel_spmd(nc, maps, core_ids=list(range(NC8)))
    y = np.concatenate([res.results[r]["y"] for r in range(NC8)], axis=0)
    return y.reshape(1, NC8 * SC, D)


# version 12 (final)
# speedup vs baseline: 1.0148x; 1.0148x over previous
import contextlib
import numpy as np
import concourse.bass as bass
import concourse.mybir as mybir
from concourse.bass_utils import run_bass_kernel_spmd
from concourse.aot_env import allow_ici_all_to_all

F32 = mybir.dt.float32
BF16 = mybir.dt.bfloat16
FP8 = mybir.dt.float8e4
U8 = mybir.dt.uint8
AF = mybir.ActivationFunctionType
A = mybir.AluOpType
AX = mybir.AxisListType.X

P = 128
SC = 1024
D = 2048
NH = 16
DH = 128
FF = 6144
NC8 = 8
C_RNE = 12582912.0
EPS = 1e-5
RG = [list(range(NC8))]
INV_D = 1.0 / D
SQ128 = float(np.sqrt(128.0))

SEMS = ["sx", "sl", "sg", "sq", "sc_", "sv", "st_", "so", "sn", "ss",
        "sea", "seb", "sla", "slb_", "sw0", "sw1", "sw2", "sw3",
        "sw4", "sw5", "sw6", "sw7", "sqa", "sqb", "scc",
        "swa", "swb", "sba", "sbb", "sfa", "sfb"]
ENGS = ["sync", "scalar", "gpsimd", "vector", "tensor"]


class Ctx:
    def __init__(self, me, eng, sems):
        self.me = me
        self.eng = eng
        self.sems = sems
        self.ctr = {n: 0 for n in SEMS}

    def op(self, engine, fn, inc=None):
        if self.me == engine:
            inst = fn(self.eng)
            if inc is not None:
                inst.then_inc(self.sems[inc[0]], inc[1])
        if inc is not None:
            self.ctr[inc[0]] += inc[1]

    def wait(self, engine, sem, lag=0, value=None):
        v = (self.ctr[sem] - lag) if value is None else value
        if v <= 0:
            return
        if self.me == engine:
            self.eng.wait_ge(self.sems[sem], v)

    def mark(self, sem):
        return self.ctr[sem]

    def chain(self, engine, fn):
        self.op(engine, fn, inc=("ss", 1))
        self.wait(engine, "ss")


def build_nc(dbg=False, phases=99):
    nc = bass.Bass("TRN2", target_bir_lowering=False)

    x_in = nc.dram_tensor("x", [SC, D], F32, kind="ExternalInput").ap()
    wqkv = nc.dram_tensor("wqkv", [D, 3 * D], F32, kind="ExternalInput").ap()
    wout = nc.dram_tensor("wout", [D, D], F32, kind="ExternalInput").ap()
    bout = nc.dram_tensor("bout", [1, D], F32, kind="ExternalInput").ap()
    gw = nc.dram_tensor("gw", [D, FF], F32, kind="ExternalInput").ap()
    uw = nc.dram_tensor("uw", [D, FF], F32, kind="ExternalInput").ap()
    dw = nc.dram_tensor("dw", [FF, D], F32, kind="ExternalInput").ap()
    gsv = nc.dram_tensor("gsv", [1, FF], F32, kind="ExternalInput").ap()
    usv = nc.dram_tensor("usv", [1, FF], F32, kind="ExternalInput").ap()
    dsv = nc.dram_tensor("dsv", [1, D], F32, kind="ExternalInput").ap()
    cvec = nc.dram_tensor("cvec", [1, D], F32, kind="ExternalInput").ap()
    wmod = nc.dram_tensor("wmod", [D, 1536], F32, kind="ExternalInput").ap()
    y_out = nc.dram_tensor("y", [SC, D], F32, kind="ExternalOutput").ap()

    mod_in = nc.dram_tensor("mod_in", [1, 1536], F32)
    mod_out = nc.dram_tensor("mod_out", [NC8, 1536], F32, addr_space="Shared")
    mx_in = nc.dram_tensor("mx_in", [P, 4], F32)
    mx_out = nc.dram_tensor("mx_out", [P, 4], F32, addr_space="Shared")
    kn_in = nc.dram_tensor("kn_in", [1, NH], F32)
    kn_out = nc.dram_tensor("kn_out", [1, NH], F32, addr_space="Shared")
    qkv_sc = nc.dram_tensor("qkv_sc", [3 * D, SC], F32)
    k8_in = nc.dram_tensor("k8_in", [D, SC], BF16)
    k8_out = nc.dram_tensor("k8_out", [NC8 * D, SC], BF16, addr_space="Shared")
    v8_in = nc.dram_tensor("v8_in", [NH * SC, DH], BF16)
    v8_out = nc.dram_tensor("v8_out", [NC8 * NH * SC, DH], BF16, addr_space="Shared")
    x2_sc = nc.dram_tensor("x2_sc", [SC, D], F32)
    g8_sc = nc.dram_tensor("g8_sc", [48, P, D], FP8)
    u8_sc = nc.dram_tensor("u8_sc", [48, P, D], FP8)
    d8_sc = nc.dram_tensor("d8_sc", [48, P, D], FP8)

    DB = {}
    if dbg:
        for nm, shape, dt in [
            ("d_mod", [NC8, 1536], F32), ("d_h1T", [P, NH * SC], BF16),
            ("d_qkv", [3 * D, SC], F32), ("d_mx", [1, 32], F32), ("d_rmax", [P, 8], F32), ("d_stat", [P, 64], F32),
            ("d_q8T", [P, NH * SC], BF16), ("d_k8o", [NC8 * D, SC], BF16),
            ("d_v8o", [NC8 * NH * SC, DH], BF16), ("d_negc", [1, 64], F32),
            ("d_v8i", [NH * SC, DH], BF16), ("d_v8late", [NC8 * NH * SC, DH], BF16),
            ("d_aoT", [P, NH * SC], BF16), ("d_x2", [SC, D], F32),
            ("d_h2T", [P, NH * SC], BF16),
        ]:
            DB[nm] = nc.dram_tensor(nm, shape, dt, kind="ExternalOutput").ap()

    O_HT, O_Q8T, O_AOT, O_XN, O_POOL = 0, 32768, 65536, 98304, 131072
    NB_POOL = 65536
    arena = nc.alloc_sbuf_tensor("arena", [P, O_POOL + NB_POOL], U8).ap()

    def view(off, dtype, cols, rows=P):
        return arena[0:rows, off:off + cols * mybir.dt.size(dtype)].bitcast(dtype)

    hT = view(O_HT, BF16, NH * SC)
    q8T = view(O_Q8T, BF16, NH * SC)
    aoT = view(O_AOT, BF16, NH * SC)
    xn = view(O_XN, F32, 8192)
    fuseT = view(O_Q8T, BF16, 3 * NH * SC)
    k8h = [view(O_XN, BF16, 8192), view(O_XN + 16384, BF16, 8192)]
    gaB = view(O_Q8T, F32, D)
    gabB = view(O_Q8T + 8192, F32, D)
    dsb2B = view(O_HT, F32, D)
    fx2all = view(O_HT + 8192, F32, 4096)
    dscr = view(O_HT + 24576, F32, 2048)

    po = [O_POOL]

    def pool(dtype, cols, rows=P, align=64):
        nbytes = cols * mybir.dt.size(dtype)
        a = (po[0] + align - 1) // align * align
        ap = arena[0:rows, a:a + nbytes].bitcast(dtype)
        po[0] = a + nbytes
        return ap

    ident32 = pool(F32, 128)
    ident16 = pool(BF16, 128)
    ones1 = pool(F32, 128, rows=1)
    onescol = pool(BF16, 1)
    sa_c = pool(F32, NH)
    ca1_c = pool(F32, NH)
    sb_c = pool(F32, NH)
    cb1_c = pool(F32, NH)
    gscol = pool(F32, 48)
    uscol = pool(F32, 48)
    rmax = pool(F32, 4)
    statb = pool(F32, 8)
    constsA = pool(F32, 16)
    negcB = pool(F32, NH)
    siluc = pool(F32, NH)
    rmxt = pool(F32, 1)
    rowbuf = pool(F32, 2048, rows=1)
    rowb2 = pool(F32, 512, rows=1)
    cimm = pool(F32, 4)
    statdump = pool(F32, 64)
    PERS_END = po[0]

    def pview(off, dtype, cols, rows=P):
        a = PERS_END + off
        nbytes = cols * mybir.dt.size(dtype)
        assert a + nbytes <= O_POOL + NB_POOL, f"pool overflow {a + nbytes}"
        return arena[0:rows, a:a + nbytes].bitcast(dtype)

    wmodb = [pview(0, F32, 1536), pview(6144, F32, 1536)]
    wqb = [pview(0, BF16, 2048), pview(4096, BF16, 2048)]
    wqb3 = [pview(24576 + j * 4096, BF16, 2048) for j in range(3)]
    stage = [pview(8192, F32, 1024), pview(12288, F32, 1024)]
    scrap = pview(16384, F32, 2048)
    wst = [view(O_XN, F32, 2048), view(O_XN + 8192, F32, 2048)]
    w8b = [view(O_XN + 16384, FP8, 2048), view(O_XN + 18432, FP8, 2048)]
    fp8buf = pview(16384, FP8, 1024)
    mainb = pview(17408, BF16, 1024)
    fixb = pview(19456, BF16, 1024)
    zb = pview(21504, F32, 1024)
    sqb = pview(25600, F32, 1024)
    mskb = pview(29696, U8, 1024)
    k8stage = pview(30720, BF16, 1024)
    v8q = pview(32768, BF16, 1024)
    v8stage = pview(34816, BF16, 1024)
    sqnb = pview(36864, BF16, 1024)
    v8h = [pview(0, BF16, 8256), pview(16512, BF16, 8256)]
    PT = [pview(33024, BF16, 1024), pview(35072, BF16, 1024)]
    attnstage = pview(37120, BF16, 512)
    rl4 = pview(38144, F32, 4)
    woutb = [pview(0, BF16, 2048), pview(4096, BF16, 2048)]
    woutb8 = [pview(i * 1024, BF16, 512) for i in range(8)]
    xsl = [pview(8192, F32, 512), pview(10240, F32, 512)]
    epis = pview(12288, F32, 512)
    x2b = [pview(14336, F32, 512), pview(18432, F32, 512)]
    opscr = pview(16384, F32, 2048)
    gwb3 = [pview(0, BF16, 2048), pview(4096, BF16, 2048), pview(8192, BF16, 2048)]
    uwb3 = [pview(12288, BF16, 2048), pview(16384, BF16, 2048), pview(20480, BF16, 2048)]
    sgbuf = pview(24576, F32, 1024)
    fepi = [pview(28672, F32, 512), pview(30720, F32, 512)]
    dwb6 = [pview(32768 + j * 1024, BF16, 512) for j in range(6)]

    ps1 = nc.alloc_psum_tensor("psA", [P, 2048], F32).ap()
    ps2 = nc.alloc_psum_tensor("psB", [P, 2048], F32).ap()
    psT = ps1[:, 1024:1536].bitcast(BF16)

    stack = contextlib.ExitStack()
    sems = {n: stack.enter_context(nc.semaphore(n)) for n in SEMS}

    HIQ, HIK, HIV, I5Q, I5K, I5V, THQ, THK, THV, CEXP, VSC, NEGC = range(12)

    def script(c: Ctx):
        c.op("gpsimd", lambda g: g.memset(ident32[:], 0.0))
        c.op("gpsimd", lambda g: g.affine_select(
            out=ident32[:], in_=ident32[:], compare_op=A.not_equal, fill=1.0,
            base=0, pattern=[[-1, 128]], channel_multiplier=1))
        c.op("gpsimd", lambda g: g.memset(ident16[:], 0.0))
        c.op("gpsimd", lambda g: g.affine_select(
            out=ident16[:], in_=ident16[:], compare_op=A.not_equal, fill=1.0,
            base=0, pattern=[[-1, 128]], channel_multiplier=1))
        c.op("gpsimd", lambda g: g.memset(ones1[:], 1.0))
        c.op("gpsimd", lambda g: g.memset(cimm[:, 0:1], EPS))
        c.op("gpsimd", lambda g: g.memset(cimm[:, 1:2], 2.0))
        c.op("gpsimd", lambda g: g.memset(onescol[:], 1.0), inc=("sq", 1))

        def dmajor(row_ap, n):
            return row_ap.rearrange("o (j p) -> (o p) j", p=P)

        c.op("sync", lambda s: s.dma_start(out=siluc[:], in_=dmajor(cvec, NH), allow_slow_non_contiguous=True), inc=("sx", 16))
        c.op("sync", lambda s: s.dma_start(out=gscol[:], in_=dmajor(gsv, 48), allow_slow_non_contiguous=True), inc=("sx", 16))
        c.op("sync", lambda s: s.dma_start(out=uscol[:], in_=dmajor(usv, 48), allow_slow_non_contiguous=True), inc=("sx", 16))
        c.wait("scalar", "sx")
        c.op("scalar", lambda s: s.activation(out=siluc[:], in_=siluc[:], func=AF.Silu),
             inc=("sc_", 1))
        c.op("vector", lambda v: v.memset(rmax[:], 0.0), inc=("sv", 1))

        for dc in range(2):
            c.op("sync", lambda s, dc=dc: s.dma_start(
                out=wmodb[dc % 2][:], in_=wmod[dc * P:(dc + 1) * P, :]),
                inc=("sea" if dc % 2 == 0 else "seb", 16))
        c.wait("tensor", "sc_")
        for dc in range(16):
            c.wait("tensor", "sea" if dc % 2 == 0 else "seb")
            for cb in range(3):
                dst = ps1[0:1, cb * 512:(cb + 1) * 512]
                c.op("tensor", lambda t, dc=dc, cb=cb, dst=dst: t.matmul(
                    out=dst, lhsT=siluc[:, dc:dc + 1],
                    rhs=wmodb[dc % 2][:, cb * 512:(cb + 1) * 512],
                    start=(dc == 0), stop=(dc == 15)),
                    inc=("st_", 1) if cb == 2 else None)
            if dc + 2 < 16:
                c.wait("sync", "st_")
                c.op("sync", lambda s, dc=dc: s.dma_start(
                    out=wmodb[dc % 2][:], in_=wmod[(dc + 2) * P:(dc + 3) * P, :]),
                    inc=("sea" if dc % 2 == 0 else "seb", 16))
        c.wait("vector", "st_")
        c.op("vector", lambda v: v.tensor_copy(out=rowbuf[0:1, 0:1536], in_=ps1[0:1, 0:1536]),
             inc=("sv", 1))
        c.wait("sync", "sv")
        c.op("sync", lambda s: s.dma_start(out=mod_in.ap(), in_=rowbuf[0:1, 0:1536]),
             inc=("sx", 16))
        c.wait("gpsimd", "sx")
        c.op("gpsimd", lambda g: g.collective_compute(
            "AllGather", A.bypass, replica_groups=RG,
            ins=[mod_in.ap().opt()], outs=[mod_out.ap().opt()]), inc=("scc", 1))
        c.wait("sync", "scc")
        modflat = mod_out.ap().rearrange("a b -> (a b)").rearrange("(o a) -> o a", o=1)
        for i, dst in [(0, sa_c), (1, ca1_c), (3, sb_c), (4, cb1_c)]:
            c.op("sync", lambda s, i=i, dst=dst: s.dma_start(
                out=dst[:], in_=dmajor(modflat[0:1, i * D:(i + 1) * D], NH),
                allow_slow_non_contiguous=True), inc=("sx", 16))
        if dbg:
            c.op("sync", lambda s: s.dma_start(out=DB["d_mod"], in_=mod_out.ap()),
                 inc=("so", 16))
        c.wait("vector", "sx")
        c.op("vector", lambda v: v.tensor_scalar(
            out=ca1_c[:], in0=ca1_c[:], scalar1=1.0, scalar2=None, op0=A.add))
        c.op("vector", lambda v: v.tensor_scalar(
            out=cb1_c[:], in0=cb1_c[:], scalar1=1.0, scalar2=None, op0=A.add),
            inc=("sv", 1))
        modv_ready = c.mark("sv")

        def emit_LN(src_rows_fn, ca1v, sav, tap=None):
            for half in range(2):
                c.wait("sync", "st_")
                c.wait("sync", "sv")
                for sc2 in range(4):
                    sc = half * 4 + sc2
                    xsl_ = xn[:, sc2 * D:(sc2 + 1) * D]
                    c.op("sync", lambda s, sc=sc, xsl_=xsl_: s.dma_start(
                        out=xsl_, in_=src_rows_fn(sc)), inc=("sn", 16))
                    c.wait("vector", "sn")
                    c.op("vector", lambda v, xsl_=xsl_: v.reduce_sum(
                        out=statb[:, 0:1], in_=xsl_, axis=AX))
                    c.op("vector", lambda v, xsl_=xsl_: v.scalar_tensor_tensor(
                        out=scrap[:], in0=xsl_, scalar=0.0, in1=xsl_,
                        op0=A.add, op1=A.mult, accum_out=statb[:, 1:2]))
                    c.chain("vector", lambda v: v.tensor_scalar(
                        out=statb[:, 2:3], in0=statb[:, 0:1], scalar1=INV_D,
                        scalar2=None, op0=A.mult))
                    c.chain("vector", lambda v: v.tensor_tensor(
                        out=statb[:, 7:8], in0=statb[:, 2:3], in1=statb[:, 2:3], op=A.mult))
                    c.op("vector", lambda v: v.scalar_tensor_tensor(
                        out=statb[:, 3:4], in0=statb[:, 1:2], scalar=INV_D,
                        in1=statb[:, 7:8], op0=A.mult, op1=A.subtract), inc=("sv", 1))
                    c.wait("scalar", "sv")
                    c.wait("scalar", "sq", value=1)
                    c.op("scalar", lambda s: s.activation(
                        out=statb[:, 4:5], in_=statb[:, 3:4], func=AF.Sqrt,
                        bias=cimm[:, 0:1]), inc=("sc_", 1))
                    c.wait("vector", "sc_")
                    c.chain("vector", lambda v: v.reciprocal(out=statb[:, 5:6], in_=statb[:, 4:5]))
                    c.chain("vector", lambda v: v.tensor_tensor(
                        out=statb[:, 6:7], in0=statb[:, 2:3], in1=statb[:, 5:6], op=A.mult))
                    c.op("vector", lambda v, xsl_=xsl_: v.tensor_scalar(
                        out=xsl_, in0=xsl_, scalar1=statb[:, 5:6], scalar2=statb[:, 6:7],
                        op0=A.mult, op1=A.subtract), inc=("sv", 1))
                    if dbg and tap is not None:
                        c.op("vector", lambda v, sc=sc: v.tensor_copy(
                            out=statdump[:, sc * 8:(sc + 1) * 8], in_=statb[:, 0:8]),
                            inc=("sv", 1))
                c.wait("tensor", "sv")
                for dc in range(16):
                    pslot = ps1[:, 0:512] if (half * 16 + dc) % 2 == 0 else ps1[:, 512:1024]
                    c.wait("tensor", "sc_", lag=1)
                    for sc2 in range(4):
                        c.op("tensor", lambda t, dc=dc, sc2=sc2, pslot=pslot: t.transpose(
                            out=pslot[:, sc2 * 128:(sc2 + 1) * 128],
                            in_=xn[:, sc2 * D + dc * 128: sc2 * D + dc * 128 + 128],
                            identity=ident32[:]),
                            inc=("st_", 1) if sc2 == 3 else None)
                    c.wait("scalar", "st_")
                    c.wait("scalar", "sv", value=modv_ready)
                    c.op("scalar", lambda s, dc=dc, half=half, pslot=pslot: s.activation(
                        out=hT[:, dc * 1024 + half * 512: dc * 1024 + half * 512 + 512],
                        in_=pslot, func=AF.Identity, scale=ca1v[:, dc:dc + 1],
                        bias=sav[:, dc:dc + 1]), inc=("sc_", 1))
            if tap is not None:
                c.wait("sync", "sc_")
                c.op("sync", lambda s: s.dma_start(out=tap, in_=hT[:]), inc=("so", 16))
                if dbg:
                    c.wait("sync", "sv")
                    c.op("sync", lambda s: s.dma_start(out=DB["d_stat"], in_=statdump[:]),
                         inc=("so", 16))

        emit_LN(lambda sc: x_in[sc * 128:(sc + 1) * 128, :], ca1_c, sa_c,
                tap=DB["d_h1T"] if dbg else None)
        h1T_done = c.mark("sc_")
        if phases <= 1:
            c.wait("sync", "so")
            return

        wq4 = wqkv.rearrange("(dc p) (oc f) -> oc p dc f", p=P, f=128)
        gw4q = gw.rearrange("(dc p) (ic f) -> ic p dc f", p=P, f=128)
        uw4q = uw.rearrange("(dc p) (ic f) -> ic p dc f", p=P, f=128)
        dw4q = dw.rearrange("(ic p) o -> ic p o", p=P)

        pq_srcs = []
        for ici in range(48):
            pq_srcs += [(gw4q[ici], g8_sc, ici), (uw4q[ici], u8_sc, ici),
                        (dw4q[ici], d8_sc, ici)]

        def pq_load(k):
            sap, _, _ = pq_srcs[k]
            par = k % 2
            lsem = "swa" if par == 0 else "swb"
            ssem = "sba" if par == 0 else "sbb"
            c.op("scalar", lambda s, sap=sap, par=par: s.dma_start(
                out=wst[par][:], in_=sap), inc=(lsem, 16))

        def prequant_step(ic):
            if ic == 0:
                pq_load(0)
                pq_load(1)
            for j in range(3):
                k = 3 * ic + j
                par = k % 2
                lsem = "swa" if par == 0 else "swb"
                ssem = "sba" if par == 0 else "sbb"
                _, dst, ici = pq_srcs[k]
                c.wait("scalar", lsem)
                c.wait("scalar", ssem)
                c.op("scalar", lambda s, par=par: s.activation(
                    out=w8b[par][:], in_=wst[par][:], func=AF.Copy), inc=("sc_", 1))
                c.wait("scalar", "sc_")
                c.op("scalar", lambda s, dst=dst, ici=ici, par=par: s.dma_start(
                    out=dst.ap()[ici], in_=w8b[par][:]), inc=(ssem, 16))
                if k + 2 < len(pq_srcs):
                    pq_load(k + 2)
        qk_sems = ["sea", "seb", "sla"]
        for oc in range(3):
            c.op("gpsimd", lambda g, oc=oc: g.dma_start(
                out=wqb3[oc % 3].rearrange("p (dc f) -> p dc f", f=128), in_=wq4[oc]),
                inc=(qk_sems[oc % 3], 16))
        c.wait("tensor", "sc_", value=h1T_done)
        svq = c.mark("sv")
        qkv_mm = {}
        regs = [ps1[:, 0:1024], ps1[:, 1024:2048], ps2[:, 0:1024], ps2[:, 1024:2048]]
        for oc in range(48):
            reg = regs[oc % 4]
            c.wait("tensor", qk_sems[oc % 3])
            if oc >= 4:
                c.wait("tensor", "sv", value=svq + (oc - 4) * 1 + 1)
            for dc in range(16):
                for sb2 in range(2):
                    c.op("tensor", lambda t, oc=oc, dc=dc, sb2=sb2, reg=reg: t.matmul(
                        out=reg[:, sb2 * 512:(sb2 + 1) * 512],
                        lhsT=wqb3[oc % 3][:, dc * 128:(dc + 1) * 128],
                        rhs=hT[:, dc * 1024 + sb2 * 512: dc * 1024 + sb2 * 512 + 512],
                        start=(dc == 0), stop=(dc == 15)),
                        inc=("st_", 1) if (dc == 15 and sb2 == 1) else None)
            qkv_mm[oc] = c.mark("st_")
            if oc + 3 < 48:
                c.wait("gpsimd", "st_", value=qkv_mm[oc])
                c.op("gpsimd", lambda g, oc=oc: g.dma_start(
                    out=wqb3[oc % 3].rearrange("p (dc f) -> p dc f", f=128), in_=wq4[oc + 3]),
                    inc=(qk_sems[oc % 3], 16))
            c.wait("vector", "st_")
            if oc >= 2:
                c.wait("vector", "sqa" if oc % 2 == 0 else "sqb")
            col = oc // 16
            c.op("vector", lambda v, oc=oc, reg=reg: v.tensor_copy(
                out=stage[oc % 2][:], in_=reg))
            c.chain("vector", lambda v, oc=oc: v.tensor_reduce(
                out=rmxt[:], in_=stage[oc % 2][:], axis=AX, op=A.max,
                apply_absolute_value=True))
            c.op("vector", lambda v, col=col: v.tensor_tensor(
                out=rmax[:, col:col + 1], in0=rmax[:, col:col + 1], in1=rmxt[:],
                op=A.max), inc=("sv", 1))
            c.wait("sync", "sv")
            c.op("sync", lambda s, oc=oc: s.dma_start(
                out=qkv_sc.ap()[oc * 128:(oc + 1) * 128, :], in_=stage[oc % 2][:]),
                inc=("sqa" if oc % 2 == 0 else "sqb", 16))
            prequant_step(oc)
        c.wait("sync", "sqa")
        c.wait("sync", "sqb")
        if dbg:
            c.op("sync", lambda s: s.dma_start(out=DB["d_qkv"], in_=qkv_sc.ap()),
                 inc=("so", 16))
        if phases <= 2:
            c.wait("sync", "so")
            return

        c.wait("sync", "sv")
        c.op("sync", lambda s: s.dma_start(out=mx_in.ap(), in_=rmax[:]), inc=("sx", 16))
        c.wait("gpsimd", "sx")
        c.op("gpsimd", lambda g: g.collective_compute(
            "AllReduce", A.max, replica_groups=RG,
            ins=[mx_in.ap().opt()], outs=[mx_out.ap().opt()]), inc=("scc", 1))
        c.wait("sync", "scc")
        for col in range(3):
            c.op("sync", lambda s, col=col: s.dma_start(
                out=rowbuf[0:1, col * 128:(col + 1) * 128],
                in_=mx_out.ap().transpose([1, 0])[col:col + 1, :],
                allow_slow_non_contiguous=True), inc=("sx", 16))
        c.wait("vector", "sx")
        for col in range(3):
            c.chain("vector", lambda v, col=col: v.reduce_max(
                out=rowb2[0:1, col:col + 1], in_=rowbuf[0:1, col * 128:(col + 1) * 128],
                axis=AX))
        CR = 16
        c.chain("vector", lambda v: v.reciprocal(out=rowb2[0:1, 4:7], in_=rowb2[0:1, 0:3]))
        c.op("vector", lambda v: v.tensor_scalar(
            out=rowb2[0:1, CR + HIQ:CR + HIQ + 3], in0=rowb2[0:1, 4:7], scalar1=224.0,
            scalar2=None, op0=A.mult))
        c.op("vector", lambda v: v.tensor_scalar(
            out=rowb2[0:1, CR + I5Q:CR + I5Q + 3], in0=rowb2[0:1, 4:7], scalar1=229376.0,
            scalar2=None, op0=A.mult))
        c.chain("vector", lambda v: v.tensor_scalar(
            out=rowb2[0:1, 8:11], in0=rowb2[0:1, 0:3], scalar1=1.0 / 14336.0,
            scalar2=None, op0=A.mult))
        c.op("vector", lambda v: v.tensor_tensor(
            out=rowb2[0:1, CR + THQ:CR + THQ + 3], in0=rowb2[0:1, 8:11],
            in1=rowb2[0:1, 8:11], op=A.mult))
        c.chain("vector", lambda v: v.tensor_tensor(
            out=rowb2[0:1, 11:12], in0=rowb2[0:1, 0:1], in1=rowb2[0:1, 1:2], op=A.mult))
        c.chain("vector", lambda v: v.tensor_scalar(
            out=rowb2[0:1, CR + CEXP:CR + CEXP + 1], in0=rowb2[0:1, 11:12],
            scalar1=1.0 / (448.0 * 448.0 * SQ128), scalar2=None, op0=A.mult))
        c.op("vector", lambda v: v.tensor_scalar(
            out=rowb2[0:1, CR + VSC:CR + VSC + 1], in0=rowb2[0:1, 2:3],
            scalar1=1.0 / 448.0, scalar2=None, op0=A.mult))
        c.op("vector", lambda v: v.tensor_scalar(
            out=rowb2[0:1, CR + NEGC:CR + NEGC + 1],
            in0=rowb2[0:1, CR + CEXP:CR + CEXP + 1],
            scalar1=-1.0, scalar2=None, op0=A.mult), inc=("sv", 1))
        c.wait("tensor", "sv")
        c.op("tensor", lambda t: t.matmul(
            out=ps1[:, 0:16], lhsT=ones1[:], rhs=rowb2[0:1, CR:CR + 16],
            start=True, stop=True), inc=("st_", 1))
        c.wait("vector", "st_")
        c.op("vector", lambda v: v.tensor_copy(out=constsA[:], in_=ps1[:, 0:16]),
             inc=("sv", 1))
        if dbg:
            c.wait("sync", "sv")
            c.op("sync", lambda s: s.dma_start(out=DB["d_mx"], in_=rowb2[0:1, 0:32]),
                 inc=("so", 16))
            c.op("sync", lambda s: s.dma_start(out=DB["d_rmax"].tensor.ap()[:, 0:4], in_=rmax[:]),
                 inc=("so", 16))
            c.op("sync", lambda s: s.dma_start(out=DB["d_rmax"].tensor.ap()[:, 4:8], in_=mx_out.ap()),
                 inc=("so", 16))

        consts_ready_v = c.mark("sv")
        sxq2 = c.mark("sx")
        nrm_ps = [ps1[0:1, 0:512], ps1[0:1, 512:1024]]

        ROLE_ORDER = [1, 2, 0]
        for t_i in range(48):
            role = ROLE_ORDER[t_i // 16]
            oc = t_i % 16
            src_row = role * 16 + oc
            stg = stage[t_i % 2]
            if t_i >= 2:
                pass
            if t_i == 0:
                c.wait("sync", "sqa")
                c.wait("sync", "sqb")
            c.wait("sync", "sfa" if t_i % 2 == 0 else "sfb")
            c.op("sync", lambda s, src_row=src_row, stg=stg: s.dma_start(
                out=stg[:], in_=qkv_sc.ap()[src_row * 128:(src_row + 1) * 128, :]),
                inc=("sx", 16))
            c.wait("scalar", "sx")
            c.wait("scalar", "sv", value=consts_ready_v)
            c.op("scalar", lambda s, stg=stg, role=role: s.activation(
                out=fp8buf[:], in_=stg, func=AF.Copy, scale=constsA[:, HIQ + role:HIQ + role + 1]),
                inc=("sfa" if t_i % 2 == 0 else "sfb", 1))
            c.op("scalar", lambda s: s.activation(
                out=mainb[:], in_=fp8buf[:], func=AF.Copy, scale=cimm[:, 1:2]),
                inc=("sc_", 1))
            wm_s = c.mark("sc_")
            c.wait("tensor", "sc_", value=wm_s)
            c.op("tensor", lambda t: t.transpose(
                out=ps2[:, 0:128], in_=ident32[:], identity=ident32[:]))
            c.wait("vector", "sx")
            c.wait("vector", "sv", value=consts_ready_v)
            c.op("vector", lambda v, stg=stg, role=role: v.tensor_scalar(
                out=zb[:], in0=stg, scalar1=constsA[:, I5Q + role:I5Q + role + 1],
                scalar2=C_RNE, op0=A.mult, op1=A.add))
            c.op("vector", lambda v: v.tensor_scalar(
                out=fixb[:], in0=zb[:], scalar1=C_RNE, scalar2=1.0 / 512.0,
                op0=A.subtract, op1=A.mult))
            c.op("vector", lambda v, stg=stg: v.tensor_tensor(
                out=sqb[:], in0=stg, in1=stg, op=A.mult),
                inc=("sfa" if t_i % 2 == 0 else "sfb", 1))
            c.op("vector", lambda v, role=role: v.tensor_scalar(
                out=mskb[:], in0=sqb[:], scalar1=constsA[:, THQ + role:THQ + role + 1],
                scalar2=None, op0=A.is_lt))
            c.wait("vector", "sc_")
            dest = (q8T[:, oc * 1024:(oc + 1) * 1024] if role == 0
                    else (k8stage[:] if role == 1 else v8q[:]))
            c.op("vector", lambda v, dest=dest: v.select(
                out=dest, mask=mskb[:], on_true=fixb[:], on_false=mainb[:]),
                inc=("sv", 1))
            wm_v = c.mark("sv")
            c.wait("tensor", "sv", value=wm_v)
            c.op("tensor", lambda t: t.transpose(
                out=ps2[:, 128:256], in_=ident32[:], identity=ident32[:]))
            if role == 0:
                c.wait("scalar", "sv")
                c.op("scalar", lambda s, oc=oc: s.activation(
                    out=sqnb[:], in_=q8T[:, oc * 1024:(oc + 1) * 1024], func=AF.Square),
                    inc=("sc_", 1))
                c.wait("tensor", "sc_")
                c.wait("tensor", "sv", lag=0)
                for hf in range(2):
                    c.op("tensor", lambda t, hf=hf: t.matmul(
                        out=nrm_ps[hf], lhsT=onescol[:],
                        rhs=sqnb[:, hf * 512:(hf + 1) * 512], start=True, stop=True),
                        inc=("st_", 1) if hf == 1 else None)
                c.wait("vector", "st_")
                c.op("vector", lambda v: v.reduce_max(
                    out=rowb2[0:1, 40:41], in_=ps1[0:1, 0:512], axis=AX))
                c.chain("vector", lambda v: v.reduce_max(
                    out=rowb2[0:1, 41:42], in_=ps1[0:1, 512:1024], axis=AX))
                c.op("vector", lambda v, oc=oc: v.tensor_tensor(
                    out=rowb2[0:1, 42 + oc:43 + oc], in0=rowb2[0:1, 40:41],
                    in1=rowb2[0:1, 41:42], op=A.max), inc=("sv", 1))
            elif role == 1:
                c.wait("sync", "sv")
                c.op("sync", lambda s, oc=oc: s.dma_start(
                    out=k8_in.ap()[oc * 128:(oc + 1) * 128, :], in_=k8stage[:]),
                    inc=("sx", 16))
                c.wait("scalar", "sv")
                c.op("scalar", lambda s: s.activation(
                    out=sqnb[:], in_=k8stage[:], func=AF.Square), inc=("sc_", 1))
                c.wait("tensor", "sc_")
                for hf in range(2):
                    c.op("tensor", lambda t, hf=hf: t.matmul(
                        out=nrm_ps[hf], lhsT=onescol[:],
                        rhs=sqnb[:, hf * 512:(hf + 1) * 512], start=True, stop=True),
                        inc=("st_", 1) if hf == 1 else None)
                c.wait("vector", "st_")
                c.op("vector", lambda v: v.reduce_max(
                    out=rowb2[0:1, 40:41], in_=ps1[0:1, 0:512], axis=AX))
                c.chain("vector", lambda v: v.reduce_max(
                    out=rowb2[0:1, 41:42], in_=ps1[0:1, 512:1024], axis=AX))
                c.op("vector", lambda v, oc=oc: v.tensor_tensor(
                    out=rowb2[0:1, 58 + oc:59 + oc], in0=rowb2[0:1, 40:41],
                    in1=rowb2[0:1, 41:42], op=A.max), inc=("sv", 1))
                if oc == 15:
                    c.wait("sync", "sx")
                    c.op("sync", lambda s: s.dma_start(
                        out=kn_in.ap(), in_=rowb2[0:1, 58:74]), inc=("sx", 16))
                    c.wait("gpsimd", "sx")
                    c.op("gpsimd", lambda g: g.collective_compute(
                        "AllGather", A.bypass, replica_groups=RG,
                        ins=[k8_in.ap().opt()], outs=[k8_out.ap().opt()]), inc=("scc", 1))
                    c.op("gpsimd", lambda g: g.collective_compute(
                        "AllReduce", A.max, replica_groups=RG,
                        ins=[kn_in.ap().opt()], outs=[kn_out.ap().opt()]), inc=("scc", 1))
            else:
                c.wait("tensor", "sv")
                c.wait("tensor", "sc_", lag=0)
                for j in range(8):
                    c.op("tensor", lambda t, j=j: t.transpose(
                        out=psT[:, j * 128:(j + 1) * 128],
                        in_=v8q[:, j * 128:(j + 1) * 128], identity=ident16[:]),
                        inc=("st_", 1) if j == 7 else None)
                c.wait("vector", "st_")
                c.wait("vector", "sx")
                c.op("vector", lambda v: v.tensor_copy(out=v8stage[:], in_=psT[:]),
                     inc=("sv", 1))
                c.wait("sync", "sv")
                c.op("sync", lambda s, oc=oc: s.dma_start(
                    out=v8_in.ap()[oc * 1024:(oc + 1) * 1024, :].rearrange(
                        "(j p) f -> p j f", p=P),
                    in_=v8stage.rearrange("p (j f) -> p j f", f=128)), inc=("sx", 16))
        c.wait("gpsimd", "sx")
        c.op("gpsimd", lambda g: g.collective_compute(
            "AllGather", A.bypass, replica_groups=RG,
            ins=[v8_in.ap().opt()], outs=[v8_out.ap().opt()]), inc=("scc", 1))
        AG_done = c.mark("scc")

        c.wait("sync", "scc", value=AG_done - 1)
        c.op("sync", lambda s: s.dma_start(out=rowb2[0:1, 74:90], in_=kn_out.ap()),
             inc=("sx", 16))
        c.wait("vector", "sx")
        c.op("vector", lambda v: v.tensor_tensor(
            out=rowb2[0:1, 90:106], in0=rowb2[0:1, 42:58], in1=rowb2[0:1, 74:90],
            op=A.mult), inc=("sv", 1))
        c.wait("vector", "sv")
        c.wait("scalar", "sv")
        c.op("scalar", lambda s: s.activation(
            out=rowb2[0:1, 106:122], in_=rowb2[0:1, 90:106], func=AF.Sqrt),
            inc=("sc_", 1))
        c.wait("vector", "sc_")
        c.op("vector", lambda v: v.tensor_scalar(
            out=rowb2[0:1, 122:138], in0=rowb2[0:1, 106:122],
            scalar1=rowb2[0:1, CR + NEGC:CR + NEGC + 1], scalar2=None, op0=A.mult),
            inc=("sv", 1))
        c.wait("tensor", "sv")
        c.op("tensor", lambda t: t.matmul(
            out=ps1[:, 16:32], lhsT=ones1[:], rhs=rowb2[0:1, 122:138],
            start=True, stop=True), inc=("st_", 1))
        c.wait("vector", "st_")
        c.op("vector", lambda v: v.tensor_copy(out=negcB[:], in_=ps1[:, 16:32]),
             inc=("sv", 1))
        attn_consts_v = c.mark("sv")
        if dbg:
            c.wait("sync", "sv")
            c.op("sync", lambda s: s.dma_start(out=DB["d_negc"], in_=rowb2[0:1, 74:138]),
                 inc=("so", 16))
            c.op("sync", lambda s: s.dma_start(out=DB["d_q8T"], in_=q8T[:]), inc=("so", 16))
            c.wait("sync", "scc", value=AG_done)
            c.op("sync", lambda s: s.dma_start(out=DB["d_k8o"], in_=k8_out.ap()), inc=("so", 16))
            c.op("sync", lambda s: s.dma_start(out=DB["d_v8o"], in_=v8_out.ap()), inc=("so", 16))
            c.op("sync", lambda s: s.dma_start(out=DB["d_v8i"], in_=v8_in.ap()), inc=("so", 16))

        if phases <= 4:
            c.wait("sync", "so")
            return
        c.wait("gpsimd", "sv")
        for b in range(2):
            c.op("gpsimd", lambda g, b=b: g.memset(
                v8h[b].rearrange("p (kc e) -> p kc e", e=129)[:, :, 128:129], 1.0),
                inc=("sq", 1))
        def head_loads(h):
            buf = h % 2
            hs = "sla" if buf == 0 else "slb_"
            for r in range(NC8):
                c.op("scalar", lambda s, h=h, r=r, buf=buf: s.dma_start(
                    out=k8h[buf][:, r * 1024:(r + 1) * 1024],
                    in_=k8_out.ap()[r * D + h * 128: r * D + (h + 1) * 128, :]),
                    inc=(hs, 16))
            for r in range(NC8):
                c.op("scalar", lambda s, h=h, r=r, buf=buf: s.dma_start(
                    out=v8h[buf].rearrange("p (kc e) -> p kc e", e=129)[:, r * 8:(r + 1) * 8, 0:128],
                    in_=v8_out.ap()[r * NH * SC + h * SC: r * NH * SC + (h + 1) * SC, :]
                    .rearrange("(j p) f -> p j f", p=P)), inc=(hs, 16))

        c.wait("scalar", "scc", value=AG_done)
        c.wait("scalar", "sq")
        c.wait("scalar", "sc_")
        head_loads(0)
        head_loads(1)
        psS = [ps1[:, 0:1024], ps2[:, 0:1024], ps1[:, 1024:2048]]
        pvt = [ps2[:, 1024 + 256 * qs: 1024 + 256 * qs + 129] for qs in range(4)]
        for h in range(NH):
            c.wait("tensor", "sla" if h % 2 == 0 else "slb_")
            for Qb in range(2):
                c.wait("tensor", "sv")
                sc_done = {}
                ex_done = {}
                pv_done = {}

                def scores_g(g):
                    par = g % 3
                    if g - 3 >= 0:
                        c.wait("tensor", "sc_", value=ex_done[g - 3])
                    for kcl in range(2):
                        kc = 2 * g + kcl
                        c.op("tensor", lambda t, h=h, Qb=Qb, kc=kc, kcl=kcl, par=par: t.matmul(
                            out=psS[par][:, kcl * 512:(kcl + 1) * 512],
                            lhsT=k8h[h % 2][:, kc * 128:(kc + 1) * 128],
                            rhs=q8T[:, h * 1024 + Qb * 512: h * 1024 + Qb * 512 + 512],
                            start=True, stop=True),
                            inc=("st_", 1) if kcl == 1 else None)
                    sc_done[g] = c.mark("st_")

                scores_g(0)
                scores_g(1)
                for g in range(32):
                    par = g % 3
                    wv = sc_done[g]
                    if g - 2 >= 0:
                        wv = max(wv, pv_done[g - 2])
                    c.wait("scalar", "st_", value=wv)
                    c.wait("scalar", "sv", value=attn_consts_v)
                    c.op("scalar", lambda s, h=h, par=par: s.activation(
                        out=PT[g % 2][:], in_=psS[par], func=AF.Exp,
                        scale=constsA[:, CEXP:CEXP + 1], bias=negcB[:, h:h + 1]),
                        inc=("sc_", 1))
                    ex_done[g] = c.mark("sc_")
                    if g + 2 < 32:
                        scores_g(g + 2)
                    c.wait("tensor", "sc_", value=ex_done[g])
                    for kcl in range(2):
                        for qs in range(4):
                            first = (g == 0 and kcl == 0)
                            c.op("tensor", lambda t, kcl=kcl, qs=qs, first=first, g=g: t.matmul(
                                out=pvt[qs],
                                lhsT=PT[g % 2][:, kcl * 512 + qs * 128: kcl * 512 + qs * 128 + 128],
                                rhs=v8h[h % 2][:, (2 * g + kcl) * 129: (2 * g + kcl) * 129 + 129],
                                start=(first and qs % 2 == 0), stop=(g == 31 and kcl == 1),
                                skip_group_check=True),
                                inc=("st_", 1) if (kcl == 1 and qs == 3) else None)
                    pv_done[g] = c.mark("st_")
                c.wait("vector", "st_")
                for qs in range(3):
                    c.op("vector", lambda v, qs=qs: v.reciprocal(
                        out=rl4[:, qs:qs + 1], in_=pvt[qs][:, 128:129]))
                c.chain("vector", lambda v: v.reciprocal(
                    out=rl4[:, 3:4], in_=pvt[3][:, 128:129]))
                for qs in range(4):
                    c.op("vector", lambda v, qs=qs: v.tensor_scalar(
                        out=attnstage[:, qs * 128:(qs + 1) * 128], in0=pvt[qs][:, 0:128],
                        scalar1=rl4[:, qs:qs + 1], scalar2=constsA[:, VSC:VSC + 1],
                        op0=A.mult, op1=A.mult), inc=("sv", 1) if qs == 3 else None)
                c.wait("tensor", "sv")
                for qs in range(4):
                    c.op("tensor", lambda t, qs=qs: t.transpose(
                        out=psT[:, qs * 128:(qs + 1) * 128],
                        in_=attnstage[:, qs * 128:(qs + 1) * 128], identity=ident16[:]),
                        inc=("st_", 1) if qs == 3 else None)
                c.wait("vector", "st_")
                c.op("vector", lambda v, h=h, Qb=Qb: v.tensor_copy(
                    out=aoT[:, h * 1024 + Qb * 512: h * 1024 + Qb * 512 + 512],
                    in_=psT[:, 0:512]), inc=("sv", 1))
            if h + 2 < NH:
                c.wait("scalar", "st_")
                head_loads(h + 2)
        if dbg:
            c.wait("sync", "sv")
            c.op("sync", lambda s: s.dma_start(out=DB["d_aoT"], in_=aoT[:]), inc=("so", 16))

        if phases <= 5:
            c.wait("sync", "so")
            return
        c.wait("sync", "sv")
        c.op("sync", lambda s: s.dma_start(
            out=rowbuf[0:1, 0:D], in_=modflat[0:1, 2 * D:3 * D]), inc=("sx", 16))
        c.wait("tensor", "sx")
        for cb in range(4):
            c.op("tensor", lambda t, cb=cb: t.matmul(
                out=ps1[:, cb * 512:(cb + 1) * 512], lhsT=ones1[:],
                rhs=rowbuf[0:1, cb * 512:(cb + 1) * 512], start=True, stop=True),
                inc=("st_", 1) if cb == 3 else None)
        c.wait("vector", "st_")
        c.op("vector", lambda v: v.tensor_copy(out=gaB[:], in_=ps1[:, 0:2048]),
             inc=("sv", 1))
        c.wait("sync", "st_")
        c.op("sync", lambda s: s.dma_start(out=rowbuf[0:1, 0:D], in_=bout), inc=("sx", 16))
        c.wait("tensor", "sx")
        c.wait("tensor", "sv")
        for cb in range(4):
            c.op("tensor", lambda t, cb=cb: t.matmul(
                out=ps1[:, cb * 512:(cb + 1) * 512], lhsT=ones1[:],
                rhs=rowbuf[0:1, cb * 512:(cb + 1) * 512], start=True, stop=True),
                inc=("st_", 1) if cb == 3 else None)
        c.op("vector", lambda v: v.tensor_scalar(
            out=opscr[:], in0=gaB[:], scalar1=0.0, scalar2=None, op0=A.is_ge))
        c.op("vector", lambda v: v.tensor_scalar(
            out=opscr[:], in0=opscr[:], scalar1=2.0, scalar2=1.0, op0=A.mult,
            op1=A.subtract))
        c.op("vector", lambda v: v.tensor_scalar(
            out=gabB[:], in0=gaB[:], scalar1=-1.0, scalar2=None, op0=A.mult))
        c.op("vector", lambda v: v.tensor_tensor(
            out=gaB[:], in0=gaB[:], in1=gabB[:], op=A.max))
        c.op("vector", lambda v: v.tensor_scalar(
            out=gaB[:], in0=gaB[:], scalar1=0.25, scalar2=None, op0=A.max))
        c.op("vector", lambda v: v.tensor_tensor(
            out=gaB[:], in0=gaB[:], in1=opscr[:], op=A.mult), inc=("sv", 1))
        c.wait("vector", "st_")
        c.op("vector", lambda v: v.tensor_tensor(
            out=gabB[:], in0=gaB[:], in1=ps1[:, 0:2048], op=A.mult), inc=("sv", 1))
        gab_ready = c.mark("sv")

        wo4 = wout.rearrange("(hc p) o -> hc p o", p=P)
        units = [(ob, sh) for ob in range(4) for sh in range(2)]
        banks = [ps1[:, i * 512:(i + 1) * 512] for i in range(4)] + \
                [ps2[:, i * 512:(i + 1) * 512] for i in range(4)]
        c.wait("tensor", "sv")
        c.wait("gpsimd", "st_")
        c.wait("sync", "st_")
        svo = c.mark("sv")
        wo_ld = {}
        wo_mm = {}
        for i, (ob, sh) in enumerate(units):
            par = i % 2
            bgrp = banks[0:4] if par == 0 else banks[4:8]
            for hc in range(8):
                k = i * 16 + hc
                if k - 8 in wo_mm:
                    c.wait("gpsimd", "st_", value=wo_mm[k - 8])
                c.op("gpsimd", lambda g, ob=ob, hc=hc: g.dma_start(
                    out=woutb8[hc % 8][:],
                    in_=wo4[hc][:, ob * 512:(ob + 1) * 512]), inc=(f"sw{hc % 8}", 16))
            if i >= 2:
                c.wait("tensor", "sv", value=svo + (i - 1) * 4)
            for hc in range(NH):
                k = i * 16 + hc
                c.wait("tensor", f"sw{hc % 8}")
                for sc2 in range(4):
                    sc = sh * 4 + sc2
                    c.op("tensor", lambda t, hc=hc, sc=sc, sc2=sc2, bgrp=bgrp: t.matmul(
                        out=bgrp[sc2],
                        lhsT=aoT[:, hc * 1024 + sc * 128: hc * 1024 + sc * 128 + 128],
                        rhs=woutb8[hc % 8][:],
                        start=(hc == 0), stop=(hc == 15)),
                        inc=("st_", 1) if sc2 == 3 else None)
                wo_mm[k] = c.mark("st_")
                if hc + 8 < NH:
                    c.wait("gpsimd", "st_", value=wo_mm[k])
                    c.op("gpsimd", lambda g, ob=ob, hc=hc: g.dma_start(
                        out=woutb8[(hc + 8) % 8][:],
                        in_=wo4[hc + 8][:, ob * 512:(ob + 1) * 512]),
                        inc=(f"sw{(hc + 8) % 8}", 16))
            for sc2 in range(4):
                sc = sh * 4 + sc2
                c.op("sync", lambda s, sc=sc, ob=ob, sc2=sc2: s.dma_start(
                    out=xsl[sc2 % 2][:],
                    in_=x_in[sc * 128:(sc + 1) * 128, ob * 512:(ob + 1) * 512]),
                    inc=("sx", 16))
                c.wait("vector", "st_")
                c.wait("vector", "sx", lag=0)
                c.wait("vector", "sv", value=gab_ready)
                c.op("vector", lambda v, sc2=sc2, ob=ob, bgrp=bgrp: v.tensor_tensor(
                    out=epis[:], in0=bgrp[sc2], in1=gaB[:, ob * 512:(ob + 1) * 512],
                    op=A.mult))
                c.op("vector", lambda v, ob=ob: v.tensor_tensor(
                    out=epis[:], in0=epis[:], in1=gabB[:, ob * 512:(ob + 1) * 512],
                    op=A.add))
                c.op("vector", lambda v, sc2=sc2: v.tensor_tensor(
                    out=x2b[sc2 % 2][:], in0=epis[:], in1=xsl[sc2 % 2][:], op=A.add),
                    inc=("sv", 1))
                c.wait("sync", "sv")
                c.op("sync", lambda s, sc=sc, ob=ob, sc2=sc2: s.dma_start(
                    out=x2_sc.ap()[sc * 128:(sc + 1) * 128, ob * 512:(ob + 1) * 512],
                    in_=x2b[sc2 % 2][:]), inc=("sx", 16))
        x2_done_sx = c.mark("sx")
        if dbg:
            c.wait("sync", "sx", value=x2_done_sx)
            c.op("sync", lambda s: s.dma_start(out=DB["d_x2"], in_=x2_sc.ap()),
                 inc=("so", 16))

        c.wait("sync", "sx", value=x2_done_sx)
        emit_LN(lambda sc: x2_sc.ap()[sc * 128:(sc + 1) * 128, :], cb1_c, sb_c,
                tap=DB["d_h2T"] if dbg else None)
        h2T_done = c.mark("sc_")

        if phases <= 6:
            c.wait("sync", "so")
            return
        gw4 = gw.rearrange("(dc p) (ic f) -> ic p dc f", p=P, f=128)
        uw4 = uw.rearrange("(dc p) (ic f) -> ic p dc f", p=P, f=128)
        c.wait("gpsimd", "sx", value=x2_done_sx)
        c.wait("gpsimd", "sba")
        c.wait("gpsimd", "sbb")
        c.wait("gpsimd", "sc_", value=h2T_done)
        gu_sems = ["sea", "seb", "sla"]
        for ic in range(3):
            sj = gu_sems[ic % 3]
            c.op("gpsimd", lambda g, ic=ic: g.dma_start(
                out=gwb3[ic % 3][:], in_=g8_sc.ap()[ic]), inc=(sj, 16))
            c.op("gpsimd", lambda g, ic=ic: g.dma_start(
                out=uwb3[ic % 3][:], in_=u8_sc.ap()[ic]), inc=(sj, 16))
        c.wait("tensor", "sc_", value=h2T_done)
        svf = c.mark("sv")
        ffn_mm = {}
        for ic in range(48):
            reg = ps1 if ic % 2 == 0 else ps2
            c.wait("tensor", gu_sems[ic % 3])
            if ic >= 2:
                c.wait("tensor", "sv", value=svf + (ic - 2) + 1)
            for dc in range(16):
                for sb2 in range(2):
                    c.op("tensor", lambda t, ic=ic, dc=dc, sb2=sb2, reg=reg: t.matmul(
                        out=reg[:, sb2 * 512:(sb2 + 1) * 512],
                        lhsT=gwb3[ic % 3][:, dc * 128:(dc + 1) * 128],
                        rhs=hT[:, dc * 1024 + sb2 * 512: dc * 1024 + sb2 * 512 + 512],
                        start=(dc == 0), stop=(dc == 15)))
                for sb2 in range(2):
                    c.op("tensor", lambda t, ic=ic, dc=dc, sb2=sb2, reg=reg: t.matmul(
                        out=reg[:, 1024 + sb2 * 512: 1024 + (sb2 + 1) * 512],
                        lhsT=uwb3[ic % 3][:, dc * 128:(dc + 1) * 128],
                        rhs=hT[:, dc * 1024 + sb2 * 512: dc * 1024 + sb2 * 512 + 512],
                        start=(dc == 0), stop=(dc == 15)),
                        inc=("st_", 1) if (dc == 15 and sb2 == 1) else None)
            ffn_mm[ic] = c.mark("st_")
            if ic + 3 < 48:
                c.wait("gpsimd", "st_", value=ffn_mm[ic])
                sj = gu_sems[ic % 3]
                c.op("gpsimd", lambda g, ic=ic: g.dma_start(
                    out=gwb3[ic % 3][:], in_=g8_sc.ap()[ic + 3]), inc=(sj, 16))
                c.op("gpsimd", lambda g, ic=ic: g.dma_start(
                    out=uwb3[ic % 3][:], in_=u8_sc.ap()[ic + 3]), inc=(sj, 16))
            c.wait("scalar", "st_")
            c.wait("scalar", "sv", lag=0)
            c.op("scalar", lambda s, ic=ic, reg=reg: s.activation(
                out=sgbuf[:], in_=reg[:, 0:1024], func=AF.Silu,
                scale=gscol[:, ic:ic + 1]), inc=("sc_", 1))
            c.wait("vector", "sc_")
            c.op("vector", lambda v, ic=ic, reg=reg: v.scalar_tensor_tensor(
                out=fuseT[:, ic * 1024:(ic + 1) * 1024], in0=reg[:, 1024:2048],
                scalar=uscol[:, ic:ic + 1], in1=sgbuf[:], op0=A.mult, op1=A.mult),
                inc=("sv", 1))
        fuse_done = c.mark("sv")

        c.op("sync", lambda s: s.dma_start(
            out=rowbuf[0:1, 0:D], in_=modflat[0:1, 5 * D:6 * D]), inc=("sx", 16))
        c.wait("tensor", "sx")
        c.wait("tensor", "sv", value=fuse_done)
        for cb in range(4):
            c.op("tensor", lambda t, cb=cb: t.matmul(
                out=ps1[:, cb * 512:(cb + 1) * 512], lhsT=ones1[:],
                rhs=rowbuf[0:1, cb * 512:(cb + 1) * 512], start=True, stop=True),
                inc=("st_", 1) if cb == 3 else None)
        c.wait("vector", "st_")
        c.op("vector", lambda v: v.tensor_copy(out=dsb2B[:], in_=ps1[:, 0:2048]),
             inc=("sv", 1))
        c.wait("sync", "st_")
        c.op("sync", lambda s: s.dma_start(out=rowbuf[0:1, 0:D], in_=dsv), inc=("sx", 16))
        c.wait("tensor", "sx")
        c.wait("tensor", "sv")
        for cb in range(4):
            c.op("tensor", lambda t, cb=cb: t.matmul(
                out=ps1[:, cb * 512:(cb + 1) * 512], lhsT=ones1[:],
                rhs=rowbuf[0:1, cb * 512:(cb + 1) * 512], start=True, stop=True),
                inc=("st_", 1) if cb == 3 else None)
        c.op("vector", lambda v: v.tensor_scalar(
            out=dscr[:], in0=dsb2B[:], scalar1=0.0, scalar2=None, op0=A.is_ge))
        c.op("vector", lambda v: v.tensor_scalar(
            out=dscr[:], in0=dscr[:], scalar1=2.0, scalar2=1.0, op0=A.mult,
            op1=A.subtract))
        c.op("vector", lambda v: v.tensor_scalar(
            out=fx2all[:, 0:2048], in0=dsb2B[:], scalar1=-1.0, scalar2=None, op0=A.mult))
        c.op("vector", lambda v: v.tensor_tensor(
            out=dsb2B[:], in0=dsb2B[:], in1=fx2all[:, 0:2048], op=A.max))
        c.op("vector", lambda v: v.tensor_scalar(
            out=dsb2B[:], in0=dsb2B[:], scalar1=0.25, scalar2=None, op0=A.max))
        c.op("vector", lambda v: v.tensor_tensor(
            out=dsb2B[:], in0=dsb2B[:], in1=dscr[:], op=A.mult))
        c.wait("vector", "st_")
        c.op("vector", lambda v: v.tensor_tensor(
            out=dsb2B[:], in0=dsb2B[:], in1=ps1[:, 0:2048], op=A.mult), inc=("sv", 1))
        dsb2_ready = c.mark("sv")

        dw4 = dw.rearrange("(ic p) o -> ic p o", p=P)
        dw_sems = ["sw0", "sw1", "sw2", "sw3", "sw4", "sw5"]
        dmm = {}
        for m in range(6):
            c.op("gpsimd", lambda g, m=m: g.dma_start(
                out=dwb6[m % 6][:], in_=d8_sc.ap()[m % 48][:, (m // 48) * 512:(m // 48) * 512 + 512]),
                inc=(dw_sems[m % 6], 16))
        c.wait("tensor", "sv", value=dsb2_ready)
        sob = c.mark("so")
        for ob in range(4):
            c.wait("sync", "st_")
            c.wait("sync", "sv")
            sxd = c.mark("sx")
            for sc in range(8):
                c.op("sync", lambda s, sc=sc, ob=ob: s.dma_start(
                    out=fx2all[:, sc * 512:(sc + 1) * 512],
                    in_=x2_sc.ap()[sc * 128:(sc + 1) * 128, ob * 512:(ob + 1) * 512]),
                    inc=("sx", 16))
            if ob > 0:
                c.wait("tensor", "sv")
            for ic in range(48):
                m = ob * 48 + ic
                c.wait("tensor", dw_sems[m % 6])
                for sc in range(8):
                    c.op("tensor", lambda t, ic=ic, sc=sc, m=m: t.matmul(
                        out=banks[sc],
                        lhsT=fuseT[:, ic * 1024 + sc * 128: ic * 1024 + sc * 128 + 128],
                        rhs=dwb6[m % 6][:],
                        start=(ic == 0), stop=(ic == 47)),
                        inc=("st_", 1) if sc == 7 else None)
                dmm[m] = c.mark("st_")
                if m + 6 < 192:
                    m2 = m + 6
                    c.wait("gpsimd", "st_", value=dmm[m])
                    c.op("gpsimd", lambda g, m2=m2: g.dma_start(
                        out=dwb6[m2 % 6][:],
                        in_=d8_sc.ap()[m2 % 48][:, (m2 // 48) * 512:(m2 // 48) * 512 + 512]),
                        inc=(dw_sems[m2 % 6], 16))
            c.wait("vector", "st_")
            c.wait("vector", "sx", value=sxd + 16 * 8)
            c.wait("vector", "sv", value=dsb2_ready)
            for sc in range(8):
                n_st = ob * 8 + sc
                if n_st >= 2:
                    c.wait("vector", "so", value=sob + 16 * (n_st - 1))
                c.op("vector", lambda v, sc=sc, ob=ob: v.tensor_tensor(
                    out=fepi[sc % 2][:], in0=banks[sc],
                    in1=dsb2B[:, ob * 512:(ob + 1) * 512], op=A.mult))
                c.op("vector", lambda v, sc=sc: v.tensor_tensor(
                    out=fepi[sc % 2][:], in0=fepi[sc % 2][:],
                    in1=fx2all[:, sc * 512:(sc + 1) * 512], op=A.add), inc=("sv", 1))
                c.wait("sync", "sv")
                c.op("sync", lambda s, sc=sc, ob=ob: s.dma_start(
                    out=y_out[sc * 128:(sc + 1) * 128, ob * 512:(ob + 1) * 512],
                    in_=fepi[sc % 2][:]), inc=("so", 16))
        if dbg:
            c.op("sync", lambda s: s.dma_start(out=DB["d_v8late"], in_=v8_out.ap()),
                 inc=("so", 16))
        c.wait("sync", "so")

    with nc.Block() as block:
        @block.sync
        def _(eng):
            script(Ctx("sync", eng, sems))

        @block.scalar
        def _(eng):
            script(Ctx("scalar", eng, sems))

        @block.gpsimd
        def _(eng):
            with allow_ici_all_to_all():
                script(Ctx("gpsimd", eng, sems))

        @block.vector
        def _(eng):
            script(Ctx("vector", eng, sems))

        @block.tensor
        def _(eng):
            script(Ctx("tensor", eng, sems))

    stack.close()
    return nc


def make_in_maps(inputs):
    x = np.ascontiguousarray(inputs["L0_attn_x"].reshape(NC8 * SC, D))
    wqkv = np.ascontiguousarray(inputs["L0_attn_W_qkv"])
    wout = np.ascontiguousarray(inputs["L0_attn_W_out"])
    bout = np.ascontiguousarray(inputs["L0_attn_b_out"]).reshape(1, D)
    gw = np.ascontiguousarray(inputs["L0_ffn_gate_w"])
    uw = np.ascontiguousarray(inputs["L0_ffn_up_w"])
    dw = np.ascontiguousarray(inputs["L0_ffn_down_w"])
    gsv = np.ascontiguousarray(inputs["L0_ffn_gate_scale"]).reshape(1, FF)
    usv = np.ascontiguousarray(inputs["L0_ffn_up_scale"]).reshape(1, FF)
    dsv = np.ascontiguousarray(inputs["L0_ffn_down_scale"]).reshape(1, D)
    cvec = np.ascontiguousarray(inputs["L0_c"]).reshape(1, D)
    wmod = np.ascontiguousarray(inputs["L0_wmod"])
    maps = []
    for r in range(NC8):
        maps.append({
            "x": np.ascontiguousarray(x[r * SC:(r + 1) * SC]),
            "wqkv": wqkv, "wout": wout, "bout": bout,
            "gw": gw, "uw": uw, "dw": dw, "gsv": gsv, "usv": usv, "dsv": dsv,
            "cvec": cvec,
            "wmod": np.ascontiguousarray(wmod[:, r * 1536:(r + 1) * 1536]),
        })
    return maps


def kernel(**inputs):
    nc = build_nc(dbg=False)
    maps = make_in_maps(inputs)
    res = run_bass_kernel_spmd(nc, maps, core_ids=list(range(NC8)))
    y = np.concatenate([res.results[r]["y"] for r in range(NC8)], axis=0)
    return y.reshape(1, NC8 * SC, D)


# version 13
# speedup vs baseline: 1.0260x; 1.0111x over previous
import contextlib
import numpy as np
import concourse.bass as bass
import concourse.mybir as mybir
from concourse.bass_utils import run_bass_kernel_spmd
from concourse.aot_env import allow_ici_all_to_all

F32 = mybir.dt.float32
BF16 = mybir.dt.bfloat16
FP8 = mybir.dt.float8e4
U8 = mybir.dt.uint8
AF = mybir.ActivationFunctionType
A = mybir.AluOpType
AX = mybir.AxisListType.X

P = 128
SC = 1024
D = 2048
NH = 16
DH = 128
FF = 6144
NC8 = 8
C_RNE = 12582912.0
EPS = 1e-5
RG = [list(range(NC8))]
INV_D = 1.0 / D
SQ128 = float(np.sqrt(128.0))

SEMS = ["sx", "sl", "sg", "sq", "sc_", "sv", "st_", "so", "sn", "ss",
        "sea", "seb", "sla", "slb_", "sw0", "sw1", "sw2", "sw3",
        "sw4", "sw5", "sw6", "sw7", "sqa", "sqb", "scc",
        "swa", "swb", "sba", "sbb", "sfa", "sfb"]
ENGS = ["sync", "scalar", "gpsimd", "vector", "tensor"]


class Ctx:
    def __init__(self, me, eng, sems):
        self.me = me
        self.eng = eng
        self.sems = sems
        self.ctr = {n: 0 for n in SEMS}

    def op(self, engine, fn, inc=None):
        if self.me == engine:
            inst = fn(self.eng)
            if inc is not None:
                inst.then_inc(self.sems[inc[0]], inc[1])
        if inc is not None:
            self.ctr[inc[0]] += inc[1]

    def wait(self, engine, sem, lag=0, value=None):
        v = (self.ctr[sem] - lag) if value is None else value
        if v <= 0:
            return
        if self.me == engine:
            self.eng.wait_ge(self.sems[sem], v)

    def mark(self, sem):
        return self.ctr[sem]

    def chain(self, engine, fn):
        self.op(engine, fn, inc=("ss", 1))
        self.wait(engine, "ss")


def build_nc(dbg=False, phases=99):
    nc = bass.Bass("TRN2", target_bir_lowering=False)

    x_in = nc.dram_tensor("x", [SC, D], F32, kind="ExternalInput").ap()
    wqkv = nc.dram_tensor("wqkv", [D, 3 * D], F32, kind="ExternalInput").ap()
    wout = nc.dram_tensor("wout", [D, D], F32, kind="ExternalInput").ap()
    bout = nc.dram_tensor("bout", [1, D], F32, kind="ExternalInput").ap()
    gw = nc.dram_tensor("gw", [D, FF], F32, kind="ExternalInput").ap()
    uw = nc.dram_tensor("uw", [D, FF], F32, kind="ExternalInput").ap()
    dw = nc.dram_tensor("dw", [FF, D], F32, kind="ExternalInput").ap()
    gsv = nc.dram_tensor("gsv", [1, FF], F32, kind="ExternalInput").ap()
    usv = nc.dram_tensor("usv", [1, FF], F32, kind="ExternalInput").ap()
    dsv = nc.dram_tensor("dsv", [1, D], F32, kind="ExternalInput").ap()
    cvec = nc.dram_tensor("cvec", [1, D], F32, kind="ExternalInput").ap()
    wmod = nc.dram_tensor("wmod", [D, 1536], F32, kind="ExternalInput").ap()
    y_out = nc.dram_tensor("y", [SC, D], F32, kind="ExternalOutput").ap()

    mod_in = nc.dram_tensor("mod_in", [1, 1536], F32)
    mod_out = nc.dram_tensor("mod_out", [NC8, 1536], F32, addr_space="Shared")
    mx_in = nc.dram_tensor("mx_in", [P, 4], F32)
    mx_out = nc.dram_tensor("mx_out", [P, 4], F32, addr_space="Shared")
    kn_in = nc.dram_tensor("kn_in", [1, NH], F32)
    kn_out = nc.dram_tensor("kn_out", [1, NH], F32, addr_space="Shared")
    qkv_sc = nc.dram_tensor("qkv_sc", [3 * D, SC], F32)
    k8_in = nc.dram_tensor("k8_in", [D, SC], BF16)
    k8_out = nc.dram_tensor("k8_out", [NC8 * D, SC], BF16, addr_space="Shared")
    v8_in = nc.dram_tensor("v8_in", [NH * SC, DH], BF16)
    v8_out = nc.dram_tensor("v8_out", [NC8 * NH * SC, DH], BF16, addr_space="Shared")
    x2_sc = nc.dram_tensor("x2_sc", [SC, D], F32)
    g8_sc = nc.dram_tensor("g8_sc", [48, P, D], FP8)
    u8_sc = nc.dram_tensor("u8_sc", [48, P, D], FP8)
    d8_sc = nc.dram_tensor("d8_sc", [48, P, D], FP8)

    DB = {}
    if dbg:
        for nm, shape, dt in [
            ("d_mod", [NC8, 1536], F32), ("d_h1T", [P, NH * SC], BF16),
            ("d_qkv", [3 * D, SC], F32), ("d_mx", [1, 32], F32), ("d_rmax", [P, 8], F32), ("d_stat", [P, 64], F32),
            ("d_q8T", [P, NH * SC], BF16), ("d_k8o", [NC8 * D, SC], BF16),
            ("d_v8o", [NC8 * NH * SC, DH], BF16), ("d_negc", [1, 64], F32),
            ("d_v8i", [NH * SC, DH], BF16), ("d_v8late", [NC8 * NH * SC, DH], BF16),
            ("d_aoT", [P, NH * SC], BF16), ("d_x2", [SC, D], F32),
            ("d_h2T", [P, NH * SC], BF16),
        ]:
            DB[nm] = nc.dram_tensor(nm, shape, dt, kind="ExternalOutput").ap()

    O_HT, O_Q8T, O_AOT, O_XN, O_POOL = 0, 32768, 65536, 98304, 131072
    NB_POOL = 65536
    arena = nc.alloc_sbuf_tensor("arena", [P, O_POOL + NB_POOL], U8).ap()

    def view(off, dtype, cols, rows=P):
        return arena[0:rows, off:off + cols * mybir.dt.size(dtype)].bitcast(dtype)

    hT = view(O_HT, BF16, NH * SC)
    q8T = view(O_Q8T, BF16, NH * SC)
    aoT = view(O_AOT, BF16, NH * SC)
    xn = view(O_XN, F32, 8192)
    fuseT = view(O_Q8T, BF16, 3 * NH * SC)
    k8h = [view(O_XN, BF16, 8192), view(O_XN + 16384, BF16, 8192)]
    gaB = view(O_Q8T, F32, D)
    gabB = view(O_Q8T + 8192, F32, D)
    dsb2B = view(O_HT, F32, D)
    fx2all = view(O_HT + 8192, F32, 4096)
    dscr = view(O_HT + 24576, F32, 2048)

    po = [O_POOL]

    def pool(dtype, cols, rows=P, align=64):
        nbytes = cols * mybir.dt.size(dtype)
        a = (po[0] + align - 1) // align * align
        ap = arena[0:rows, a:a + nbytes].bitcast(dtype)
        po[0] = a + nbytes
        return ap

    ident32 = pool(F32, 128)
    ident16 = pool(BF16, 128)
    ones1 = pool(F32, 128, rows=1)
    onescol = pool(BF16, 1)
    sa_c = pool(F32, NH)
    ca1_c = pool(F32, NH)
    sb_c = pool(F32, NH)
    cb1_c = pool(F32, NH)
    gscol = pool(F32, 48)
    uscol = pool(F32, 48)
    rmax = pool(F32, 4)
    statb = pool(F32, 8)
    constsA = pool(F32, 16)
    negcB = pool(F32, NH)
    siluc = pool(F32, NH)
    rmxt = pool(F32, 1)
    rowbuf = pool(F32, 2048, rows=1)
    rowb2 = pool(F32, 512, rows=1)
    cimm = pool(F32, 4)
    statdump = pool(F32, 64)
    PERS_END = po[0]

    def pview(off, dtype, cols, rows=P):
        a = PERS_END + off
        nbytes = cols * mybir.dt.size(dtype)
        assert a + nbytes <= O_POOL + NB_POOL, f"pool overflow {a + nbytes}"
        return arena[0:rows, a:a + nbytes].bitcast(dtype)

    wmodb = [pview(0, F32, 1536), pview(6144, F32, 1536)]
    wqb = [pview(0, BF16, 2048), pview(4096, BF16, 2048)]
    wqb3 = [pview(24576 + j * 4096, BF16, 2048) for j in range(3)]
    stage = [pview(8192, F32, 1024), pview(12288, F32, 1024)]
    scrap = pview(16384, F32, 2048)
    wst = [view(O_XN, F32, 2048), view(O_XN + 8192, F32, 2048)]
    w8b = [view(O_XN + 16384, FP8, 2048), view(O_XN + 18432, FP8, 2048)]
    fp8buf = pview(16384, FP8, 1024)
    mainb = pview(17408, BF16, 1024)
    fixb = pview(19456, BF16, 1024)
    zb = pview(21504, F32, 1024)
    sqb = pview(25600, F32, 1024)
    mskb = pview(29696, U8, 1024)
    k8stage = pview(30720, BF16, 1024)
    v8q = pview(32768, BF16, 1024)
    v8stage = pview(34816, BF16, 1024)
    sqnb = pview(36864, BF16, 1024)
    v8h = [pview(0, BF16, 8256), pview(16512, BF16, 8256)]
    PT = [pview(33024, BF16, 1024), pview(35072, BF16, 1024)]
    attnstage = pview(37120, BF16, 512)
    rl4 = pview(38144, F32, 4)
    woutb = [pview(0, BF16, 2048), pview(4096, BF16, 2048)]
    woutb8 = [pview(i * 1024, BF16, 512) for i in range(8)]
    xsl = [pview(8192, F32, 512), pview(10240, F32, 512)]
    epis = pview(12288, F32, 512)
    x2b = [pview(14336, F32, 512), pview(18432, F32, 512)]
    opscr = pview(16384, F32, 2048)
    gwb3 = [pview(0, BF16, 2048), pview(4096, BF16, 2048), pview(8192, BF16, 2048)]
    uwb3 = [pview(12288, BF16, 2048), pview(16384, BF16, 2048), pview(20480, BF16, 2048)]
    sgbuf = pview(24576, F32, 1024)
    fepi = [pview(28672, F32, 512), pview(30720, F32, 512)]
    dwb6 = [pview(32768 + j * 1024, BF16, 512) for j in range(6)]

    ps1 = nc.alloc_psum_tensor("psA", [P, 2048], F32).ap()
    ps2 = nc.alloc_psum_tensor("psB", [P, 2048], F32).ap()
    psT = ps1[:, 1024:1536].bitcast(BF16)

    stack = contextlib.ExitStack()
    sems = {n: stack.enter_context(nc.semaphore(n)) for n in SEMS}

    HIQ, HIK, HIV, I5Q, I5K, I5V, THQ, THK, THV, CEXP, VSC, NEGC = range(12)

    def script(c: Ctx):
        c.op("gpsimd", lambda g: g.memset(ident32[:], 0.0))
        c.op("gpsimd", lambda g: g.affine_select(
            out=ident32[:], in_=ident32[:], compare_op=A.not_equal, fill=1.0,
            base=0, pattern=[[-1, 128]], channel_multiplier=1))
        c.op("gpsimd", lambda g: g.memset(ident16[:], 0.0))
        c.op("gpsimd", lambda g: g.affine_select(
            out=ident16[:], in_=ident16[:], compare_op=A.not_equal, fill=1.0,
            base=0, pattern=[[-1, 128]], channel_multiplier=1))
        c.op("gpsimd", lambda g: g.memset(ones1[:], 1.0))
        c.op("gpsimd", lambda g: g.memset(cimm[:, 0:1], EPS))
        c.op("gpsimd", lambda g: g.memset(cimm[:, 1:2], 2.0))
        c.op("gpsimd", lambda g: g.memset(onescol[:], 1.0), inc=("sq", 1))

        def dmajor(row_ap, n):
            return row_ap.rearrange("o (j p) -> (o p) j", p=P)

        c.op("sync", lambda s: s.dma_start(out=siluc[:], in_=dmajor(cvec, NH), allow_slow_non_contiguous=True), inc=("sx", 16))
        c.op("sync", lambda s: s.dma_start(out=gscol[:], in_=dmajor(gsv, 48), allow_slow_non_contiguous=True), inc=("sx", 16))
        c.op("sync", lambda s: s.dma_start(out=uscol[:], in_=dmajor(usv, 48), allow_slow_non_contiguous=True), inc=("sx", 16))
        c.wait("scalar", "sx")
        c.op("scalar", lambda s: s.activation(out=siluc[:], in_=siluc[:], func=AF.Silu),
             inc=("sc_", 1))
        c.op("vector", lambda v: v.memset(rmax[:], 0.0), inc=("sv", 1))

        for dc in range(2):
            c.op("sync", lambda s, dc=dc: s.dma_start(
                out=wmodb[dc % 2][:], in_=wmod[dc * P:(dc + 1) * P, :]),
                inc=("sea" if dc % 2 == 0 else "seb", 16))
        c.wait("tensor", "sc_")
        for dc in range(16):
            c.wait("tensor", "sea" if dc % 2 == 0 else "seb")
            for cb in range(3):
                dst = ps1[0:1, cb * 512:(cb + 1) * 512]
                c.op("tensor", lambda t, dc=dc, cb=cb, dst=dst: t.matmul(
                    out=dst, lhsT=siluc[:, dc:dc + 1],
                    rhs=wmodb[dc % 2][:, cb * 512:(cb + 1) * 512],
                    start=(dc == 0), stop=(dc == 15)),
                    inc=("st_", 1) if cb == 2 else None)
            if dc + 2 < 16:
                c.wait("sync", "st_")
                c.op("sync", lambda s, dc=dc: s.dma_start(
                    out=wmodb[dc % 2][:], in_=wmod[(dc + 2) * P:(dc + 3) * P, :]),
                    inc=("sea" if dc % 2 == 0 else "seb", 16))
        c.wait("vector", "st_")
        c.op("vector", lambda v: v.tensor_copy(out=rowbuf[0:1, 0:1536], in_=ps1[0:1, 0:1536]),
             inc=("sv", 1))
        c.wait("sync", "sv")
        c.op("sync", lambda s: s.dma_start(out=mod_in.ap(), in_=rowbuf[0:1, 0:1536]),
             inc=("sx", 16))
        c.wait("gpsimd", "sx")
        c.op("gpsimd", lambda g: g.collective_compute(
            "AllGather", A.bypass, replica_groups=RG,
            ins=[mod_in.ap().opt()], outs=[mod_out.ap().opt()]), inc=("scc", 1))
        c.wait("sync", "scc")
        modflat = mod_out.ap().rearrange("a b -> (a b)").rearrange("(o a) -> o a", o=1)
        for i, dst in [(0, sa_c), (1, ca1_c), (3, sb_c), (4, cb1_c)]:
            c.op("sync", lambda s, i=i, dst=dst: s.dma_start(
                out=dst[:], in_=dmajor(modflat[0:1, i * D:(i + 1) * D], NH),
                allow_slow_non_contiguous=True), inc=("sx", 16))
        if dbg:
            c.op("sync", lambda s: s.dma_start(out=DB["d_mod"], in_=mod_out.ap()),
                 inc=("so", 16))
        c.wait("vector", "sx")
        c.op("vector", lambda v: v.tensor_scalar(
            out=ca1_c[:], in0=ca1_c[:], scalar1=1.0, scalar2=None, op0=A.add))
        c.op("vector", lambda v: v.tensor_scalar(
            out=cb1_c[:], in0=cb1_c[:], scalar1=1.0, scalar2=None, op0=A.add),
            inc=("sv", 1))
        modv_ready = c.mark("sv")

        def emit_LN(src_rows_fn, ca1v, sav, tap=None):
            for half in range(2):
                c.wait("sync", "st_")
                c.wait("sync", "sv")
                for sc2 in range(4):
                    sc = half * 4 + sc2
                    xsl_ = xn[:, sc2 * D:(sc2 + 1) * D]
                    c.op("sync", lambda s, sc=sc, xsl_=xsl_: s.dma_start(
                        out=xsl_, in_=src_rows_fn(sc)), inc=("sn", 16))
                    c.wait("vector", "sn")
                    c.op("vector", lambda v, xsl_=xsl_: v.reduce_sum(
                        out=statb[:, 0:1], in_=xsl_, axis=AX))
                    c.op("vector", lambda v, xsl_=xsl_: v.scalar_tensor_tensor(
                        out=scrap[:], in0=xsl_, scalar=0.0, in1=xsl_,
                        op0=A.add, op1=A.mult, accum_out=statb[:, 1:2]))
                    c.chain("vector", lambda v: v.tensor_scalar(
                        out=statb[:, 2:3], in0=statb[:, 0:1], scalar1=INV_D,
                        scalar2=None, op0=A.mult))
                    c.chain("vector", lambda v: v.tensor_tensor(
                        out=statb[:, 7:8], in0=statb[:, 2:3], in1=statb[:, 2:3], op=A.mult))
                    c.op("vector", lambda v: v.scalar_tensor_tensor(
                        out=statb[:, 3:4], in0=statb[:, 1:2], scalar=INV_D,
                        in1=statb[:, 7:8], op0=A.mult, op1=A.subtract), inc=("sv", 1))
                    c.wait("scalar", "sv")
                    c.wait("scalar", "sq", value=1)
                    c.op("scalar", lambda s: s.activation(
                        out=statb[:, 4:5], in_=statb[:, 3:4], func=AF.Sqrt,
                        bias=cimm[:, 0:1]), inc=("sc_", 1))
                    c.wait("vector", "sc_")
                    c.chain("vector", lambda v: v.reciprocal(out=statb[:, 5:6], in_=statb[:, 4:5]))
                    c.chain("vector", lambda v: v.tensor_tensor(
                        out=statb[:, 6:7], in0=statb[:, 2:3], in1=statb[:, 5:6], op=A.mult))
                    c.op("vector", lambda v, xsl_=xsl_: v.tensor_scalar(
                        out=xsl_, in0=xsl_, scalar1=statb[:, 5:6], scalar2=statb[:, 6:7],
                        op0=A.mult, op1=A.subtract), inc=("sv", 1))
                    if dbg and tap is not None:
                        c.op("vector", lambda v, sc=sc: v.tensor_copy(
                            out=statdump[:, sc * 8:(sc + 1) * 8], in_=statb[:, 0:8]),
                            inc=("sv", 1))
                c.wait("tensor", "sv")
                for dc in range(16):
                    pslot = ps1[:, 0:512] if (half * 16 + dc) % 2 == 0 else ps1[:, 512:1024]
                    c.wait("tensor", "sc_", lag=1)
                    for sc2 in range(4):
                        c.op("tensor", lambda t, dc=dc, sc2=sc2, pslot=pslot: t.transpose(
                            out=pslot[:, sc2 * 128:(sc2 + 1) * 128],
                            in_=xn[:, sc2 * D + dc * 128: sc2 * D + dc * 128 + 128],
                            identity=ident32[:]),
                            inc=("st_", 1) if sc2 == 3 else None)
                    c.wait("scalar", "st_")
                    c.wait("scalar", "sv", value=modv_ready)
                    c.op("scalar", lambda s, dc=dc, half=half, pslot=pslot: s.activation(
                        out=hT[:, dc * 1024 + half * 512: dc * 1024 + half * 512 + 512],
                        in_=pslot, func=AF.Identity, scale=ca1v[:, dc:dc + 1],
                        bias=sav[:, dc:dc + 1]), inc=("sc_", 1))
            if tap is not None:
                c.wait("sync", "sc_")
                c.op("sync", lambda s: s.dma_start(out=tap, in_=hT[:]), inc=("so", 16))
                if dbg:
                    c.wait("sync", "sv")
                    c.op("sync", lambda s: s.dma_start(out=DB["d_stat"], in_=statdump[:]),
                         inc=("so", 16))

        emit_LN(lambda sc: x_in[sc * 128:(sc + 1) * 128, :], ca1_c, sa_c,
                tap=DB["d_h1T"] if dbg else None)
        h1T_done = c.mark("sc_")
        if phases <= 1:
            c.wait("sync", "so")
            return

        wq4 = wqkv.rearrange("(dc p) (oc f) -> oc p dc f", p=P, f=128)
        gw4q = gw.rearrange("(dc p) (ic f) -> ic p dc f", p=P, f=128)
        uw4q = uw.rearrange("(dc p) (ic f) -> ic p dc f", p=P, f=128)
        dw4q = dw.rearrange("(ic p) o -> ic p o", p=P)

        pq_srcs = []
        for ici in range(48):
            pq_srcs += [(gw4q[ici], g8_sc, ici), (uw4q[ici], u8_sc, ici),
                        (dw4q[ici], d8_sc, ici)]

        def pq_load(k):
            sap, _, _ = pq_srcs[k]
            par = k % 2
            lsem = "swa" if par == 0 else "swb"
            ssem = "sba" if par == 0 else "sbb"
            c.op("scalar", lambda s, sap=sap, par=par: s.dma_start(
                out=wst[par][:], in_=sap), inc=(lsem, 16))

        def prequant_step(ic):
            if ic == 0:
                pq_load(0)
                pq_load(1)
            for j in range(3):
                k = 3 * ic + j
                par = k % 2
                lsem = "swa" if par == 0 else "swb"
                ssem = "sba" if par == 0 else "sbb"
                _, dst, ici = pq_srcs[k]
                c.wait("scalar", lsem)
                c.wait("scalar", ssem)
                c.op("scalar", lambda s, par=par: s.activation(
                    out=w8b[par][:], in_=wst[par][:], func=AF.Copy), inc=("sc_", 1))
                c.wait("scalar", "sc_")
                c.op("scalar", lambda s, dst=dst, ici=ici, par=par: s.dma_start(
                    out=dst.ap()[ici], in_=w8b[par][:]), inc=(ssem, 16))
                if k + 2 < len(pq_srcs):
                    pq_load(k + 2)
        qk_sems = ["sea", "seb", "sla"]
        for oc in range(3):
            c.op("gpsimd", lambda g, oc=oc: g.dma_start(
                out=wqb3[oc % 3].rearrange("p (dc f) -> p dc f", f=128), in_=wq4[oc]),
                inc=(qk_sems[oc % 3], 16))
        c.wait("tensor", "sc_", value=h1T_done)
        svq = c.mark("sv")
        qkv_mm = {}
        regs = [ps1[:, 0:1024], ps1[:, 1024:2048], ps2[:, 0:1024], ps2[:, 1024:2048]]
        for oc in range(48):
            reg = regs[oc % 4]
            c.wait("tensor", qk_sems[oc % 3])
            if oc >= 4:
                c.wait("tensor", "sv", value=svq + (oc - 4) * 1 + 1)
            for dc in range(16):
                for sb2 in range(2):
                    c.op("tensor", lambda t, oc=oc, dc=dc, sb2=sb2, reg=reg: t.matmul(
                        out=reg[:, sb2 * 512:(sb2 + 1) * 512],
                        lhsT=wqb3[oc % 3][:, dc * 128:(dc + 1) * 128],
                        rhs=hT[:, dc * 1024 + sb2 * 512: dc * 1024 + sb2 * 512 + 512],
                        start=(dc == 0), stop=(dc == 15)),
                        inc=("st_", 1) if (dc == 15 and sb2 == 1) else None)
            qkv_mm[oc] = c.mark("st_")
            if oc + 3 < 48:
                c.wait("gpsimd", "st_", value=qkv_mm[oc])
                c.op("gpsimd", lambda g, oc=oc: g.dma_start(
                    out=wqb3[oc % 3].rearrange("p (dc f) -> p dc f", f=128), in_=wq4[oc + 3]),
                    inc=(qk_sems[oc % 3], 16))
            c.wait("vector", "st_")
            if oc >= 2:
                c.wait("vector", "sqa" if oc % 2 == 0 else "sqb")
            col = oc // 16
            c.op("vector", lambda v, oc=oc, reg=reg: v.tensor_copy(
                out=stage[oc % 2][:], in_=reg))
            c.chain("vector", lambda v, oc=oc: v.tensor_reduce(
                out=rmxt[:], in_=stage[oc % 2][:], axis=AX, op=A.max,
                apply_absolute_value=True))
            c.op("vector", lambda v, col=col: v.tensor_tensor(
                out=rmax[:, col:col + 1], in0=rmax[:, col:col + 1], in1=rmxt[:],
                op=A.max), inc=("sv", 1))
            c.wait("sync", "sv")
            c.op("sync", lambda s, oc=oc: s.dma_start(
                out=qkv_sc.ap()[oc * 128:(oc + 1) * 128, :], in_=stage[oc % 2][:]),
                inc=("sqa" if oc % 2 == 0 else "sqb", 16))
            prequant_step(oc)
        c.wait("sync", "sqa")
        c.wait("sync", "sqb")
        if dbg:
            c.op("sync", lambda s: s.dma_start(out=DB["d_qkv"], in_=qkv_sc.ap()),
                 inc=("so", 16))
        if phases <= 2:
            c.wait("sync", "so")
            return

        c.wait("sync", "sv")
        c.op("sync", lambda s: s.dma_start(out=mx_in.ap(), in_=rmax[:]), inc=("sx", 16))
        c.wait("gpsimd", "sx")
        c.op("gpsimd", lambda g: g.collective_compute(
            "AllReduce", A.max, replica_groups=RG,
            ins=[mx_in.ap().opt()], outs=[mx_out.ap().opt()]), inc=("scc", 1))
        c.wait("sync", "scc")
        for col in range(3):
            c.op("sync", lambda s, col=col: s.dma_start(
                out=rowbuf[0:1, col * 128:(col + 1) * 128],
                in_=mx_out.ap().transpose([1, 0])[col:col + 1, :],
                allow_slow_non_contiguous=True), inc=("sx", 16))
        c.wait("vector", "sx")
        for col in range(3):
            c.chain("vector", lambda v, col=col: v.reduce_max(
                out=rowb2[0:1, col:col + 1], in_=rowbuf[0:1, col * 128:(col + 1) * 128],
                axis=AX))
        CR = 16
        c.chain("vector", lambda v: v.reciprocal(out=rowb2[0:1, 4:7], in_=rowb2[0:1, 0:3]))
        c.op("vector", lambda v: v.tensor_scalar(
            out=rowb2[0:1, CR + HIQ:CR + HIQ + 3], in0=rowb2[0:1, 4:7], scalar1=224.0,
            scalar2=None, op0=A.mult))
        c.op("vector", lambda v: v.tensor_scalar(
            out=rowb2[0:1, CR + I5Q:CR + I5Q + 3], in0=rowb2[0:1, 4:7], scalar1=229376.0,
            scalar2=None, op0=A.mult))
        c.chain("vector", lambda v: v.tensor_scalar(
            out=rowb2[0:1, 8:11], in0=rowb2[0:1, 0:3], scalar1=1.0 / 14336.0,
            scalar2=None, op0=A.mult))
        c.op("vector", lambda v: v.tensor_tensor(
            out=rowb2[0:1, CR + THQ:CR + THQ + 3], in0=rowb2[0:1, 8:11],
            in1=rowb2[0:1, 8:11], op=A.mult))
        c.chain("vector", lambda v: v.tensor_tensor(
            out=rowb2[0:1, 11:12], in0=rowb2[0:1, 0:1], in1=rowb2[0:1, 1:2], op=A.mult))
        c.chain("vector", lambda v: v.tensor_scalar(
            out=rowb2[0:1, CR + CEXP:CR + CEXP + 1], in0=rowb2[0:1, 11:12],
            scalar1=1.0 / (448.0 * 448.0 * SQ128), scalar2=None, op0=A.mult))
        c.op("vector", lambda v: v.tensor_scalar(
            out=rowb2[0:1, CR + VSC:CR + VSC + 1], in0=rowb2[0:1, 2:3],
            scalar1=1.0 / 448.0, scalar2=None, op0=A.mult))
        c.op("vector", lambda v: v.tensor_scalar(
            out=rowb2[0:1, CR + NEGC:CR + NEGC + 1],
            in0=rowb2[0:1, CR + CEXP:CR + CEXP + 1],
            scalar1=-1.0, scalar2=None, op0=A.mult), inc=("sv", 1))
        c.wait("tensor", "sv")
        c.op("tensor", lambda t: t.matmul(
            out=ps1[:, 0:16], lhsT=ones1[:], rhs=rowb2[0:1, CR:CR + 16],
            start=True, stop=True), inc=("st_", 1))
        c.wait("vector", "st_")
        c.op("vector", lambda v: v.tensor_copy(out=constsA[:], in_=ps1[:, 0:16]),
             inc=("sv", 1))
        if dbg:
            c.wait("sync", "sv")
            c.op("sync", lambda s: s.dma_start(out=DB["d_mx"], in_=rowb2[0:1, 0:32]),
                 inc=("so", 16))
            c.op("sync", lambda s: s.dma_start(out=DB["d_rmax"].tensor.ap()[:, 0:4], in_=rmax[:]),
                 inc=("so", 16))
            c.op("sync", lambda s: s.dma_start(out=DB["d_rmax"].tensor.ap()[:, 4:8], in_=mx_out.ap()),
                 inc=("so", 16))

        consts_ready_v = c.mark("sv")
        sxq2 = c.mark("sx")
        nrm_ps = [ps1[0:1, 0:512], ps1[0:1, 512:1024]]

        ROLE_ORDER = [1, 2, 0]
        for t_i in range(48):
            role = ROLE_ORDER[t_i // 16]
            oc = t_i % 16
            src_row = role * 16 + oc
            stg = stage[t_i % 2]
            if t_i >= 2:
                pass
            if t_i == 0:
                c.wait("sync", "sqa")
                c.wait("sync", "sqb")
            c.wait("sync", "sfa" if t_i % 2 == 0 else "sfb")
            c.op("sync", lambda s, src_row=src_row, stg=stg: s.dma_start(
                out=stg[:], in_=qkv_sc.ap()[src_row * 128:(src_row + 1) * 128, :]),
                inc=("sx", 16))
            c.wait("scalar", "sx")
            c.wait("scalar", "sv", value=consts_ready_v)
            c.op("scalar", lambda s, stg=stg, role=role: s.activation(
                out=fp8buf[:], in_=stg, func=AF.Copy, scale=constsA[:, HIQ + role:HIQ + role + 1]),
                inc=("sfa" if t_i % 2 == 0 else "sfb", 1))
            c.op("scalar", lambda s: s.activation(
                out=mainb[:], in_=fp8buf[:], func=AF.Copy, scale=cimm[:, 1:2]),
                inc=("sc_", 1))
            wm_s = c.mark("sc_")
            c.wait("tensor", "sc_", value=wm_s)
            c.op("tensor", lambda t: t.transpose(
                out=ps2[:, 0:128], in_=ident32[:], identity=ident32[:]))
            c.wait("vector", "sx")
            c.wait("vector", "sv", value=consts_ready_v)
            c.op("vector", lambda v, stg=stg, role=role: v.tensor_scalar(
                out=zb[:], in0=stg, scalar1=constsA[:, I5Q + role:I5Q + role + 1],
                scalar2=C_RNE, op0=A.mult, op1=A.add))
            c.op("vector", lambda v: v.tensor_scalar(
                out=fixb[:], in0=zb[:], scalar1=C_RNE, scalar2=1.0 / 512.0,
                op0=A.subtract, op1=A.mult))
            c.op("vector", lambda v, stg=stg: v.tensor_tensor(
                out=sqb[:], in0=stg, in1=stg, op=A.mult),
                inc=("sfa" if t_i % 2 == 0 else "sfb", 1))
            c.op("vector", lambda v, role=role: v.tensor_scalar(
                out=mskb[:], in0=sqb[:], scalar1=constsA[:, THQ + role:THQ + role + 1],
                scalar2=None, op0=A.is_lt))
            c.wait("vector", "sc_")
            dest = (q8T[:, oc * 1024:(oc + 1) * 1024] if role == 0
                    else (k8stage[:] if role == 1 else v8q[:]))
            c.op("vector", lambda v, dest=dest: v.select(
                out=dest, mask=mskb[:], on_true=fixb[:], on_false=mainb[:]),
                inc=("sv", 1))
            wm_v = c.mark("sv")
            c.wait("tensor", "sv", value=wm_v)
            c.op("tensor", lambda t: t.transpose(
                out=ps2[:, 128:256], in_=ident32[:], identity=ident32[:]))
            if role == 0:
                c.wait("scalar", "sv")
                c.op("scalar", lambda s, oc=oc: s.activation(
                    out=sqnb[:], in_=q8T[:, oc * 1024:(oc + 1) * 1024], func=AF.Square),
                    inc=("sc_", 1))
                c.wait("tensor", "sc_")
                c.wait("tensor", "sv", lag=0)
                for hf in range(2):
                    c.op("tensor", lambda t, hf=hf: t.matmul(
                        out=nrm_ps[hf], lhsT=onescol[:],
                        rhs=sqnb[:, hf * 512:(hf + 1) * 512], start=True, stop=True),
                        inc=("st_", 1) if hf == 1 else None)
                c.wait("vector", "st_")
                c.op("vector", lambda v: v.reduce_max(
                    out=rowb2[0:1, 40:41], in_=ps1[0:1, 0:512], axis=AX))
                c.chain("vector", lambda v: v.reduce_max(
                    out=rowb2[0:1, 41:42], in_=ps1[0:1, 512:1024], axis=AX))
                c.op("vector", lambda v, oc=oc: v.tensor_tensor(
                    out=rowb2[0:1, 42 + oc:43 + oc], in0=rowb2[0:1, 40:41],
                    in1=rowb2[0:1, 41:42], op=A.max), inc=("sv", 1))
            elif role == 1:
                c.wait("sync", "sv")
                c.op("sync", lambda s, oc=oc: s.dma_start(
                    out=k8_in.ap()[oc * 128:(oc + 1) * 128, :], in_=k8stage[:]),
                    inc=("sx", 16))
                c.wait("scalar", "sv")
                c.op("scalar", lambda s: s.activation(
                    out=sqnb[:], in_=k8stage[:], func=AF.Square), inc=("sc_", 1))
                c.wait("tensor", "sc_")
                for hf in range(2):
                    c.op("tensor", lambda t, hf=hf: t.matmul(
                        out=nrm_ps[hf], lhsT=onescol[:],
                        rhs=sqnb[:, hf * 512:(hf + 1) * 512], start=True, stop=True),
                        inc=("st_", 1) if hf == 1 else None)
                c.wait("vector", "st_")
                c.op("vector", lambda v: v.reduce_max(
                    out=rowb2[0:1, 40:41], in_=ps1[0:1, 0:512], axis=AX))
                c.chain("vector", lambda v: v.reduce_max(
                    out=rowb2[0:1, 41:42], in_=ps1[0:1, 512:1024], axis=AX))
                c.op("vector", lambda v, oc=oc: v.tensor_tensor(
                    out=rowb2[0:1, 58 + oc:59 + oc], in0=rowb2[0:1, 40:41],
                    in1=rowb2[0:1, 41:42], op=A.max), inc=("sv", 1))
                if oc == 15:
                    c.wait("sync", "sx")
                    c.op("sync", lambda s: s.dma_start(
                        out=kn_in.ap(), in_=rowb2[0:1, 58:74]), inc=("sx", 16))
                    c.wait("gpsimd", "sx")
                    c.op("gpsimd", lambda g: g.collective_compute(
                        "AllGather", A.bypass, replica_groups=RG,
                        ins=[k8_in.ap().opt()], outs=[k8_out.ap().opt()]), inc=("scc", 1))
                    c.op("gpsimd", lambda g: g.collective_compute(
                        "AllReduce", A.max, replica_groups=RG,
                        ins=[kn_in.ap().opt()], outs=[kn_out.ap().opt()]), inc=("scc", 1))
            else:
                c.wait("tensor", "sv")
                c.wait("tensor", "sc_", lag=0)
                for j in range(8):
                    c.op("tensor", lambda t, j=j: t.transpose(
                        out=psT[:, j * 128:(j + 1) * 128],
                        in_=v8q[:, j * 128:(j + 1) * 128], identity=ident16[:]),
                        inc=("st_", 1) if j == 7 else None)
                c.wait("vector", "st_")
                c.wait("vector", "sx")
                c.op("vector", lambda v: v.tensor_copy(out=v8stage[:], in_=psT[:]),
                     inc=("sv", 1))
                c.wait("sync", "sv")
                c.op("sync", lambda s, oc=oc: s.dma_start(
                    out=v8_in.ap()[oc * 1024:(oc + 1) * 1024, :].rearrange(
                        "(j p) f -> p j f", p=P),
                    in_=v8stage.rearrange("p (j f) -> p j f", f=128)), inc=("sx", 16))
        c.wait("gpsimd", "sx")
        c.op("gpsimd", lambda g: g.collective_compute(
            "AllGather", A.bypass, replica_groups=RG,
            ins=[v8_in.ap().opt()], outs=[v8_out.ap().opt()]), inc=("scc", 1))
        AG_done = c.mark("scc")

        c.wait("sync", "scc", value=AG_done - 1)
        c.op("sync", lambda s: s.dma_start(out=rowb2[0:1, 74:90], in_=kn_out.ap()),
             inc=("sx", 16))
        c.wait("vector", "sx")
        c.op("vector", lambda v: v.tensor_tensor(
            out=rowb2[0:1, 90:106], in0=rowb2[0:1, 42:58], in1=rowb2[0:1, 74:90],
            op=A.mult), inc=("sv", 1))
        c.wait("vector", "sv")
        c.wait("scalar", "sv")
        c.op("scalar", lambda s: s.activation(
            out=rowb2[0:1, 106:122], in_=rowb2[0:1, 90:106], func=AF.Sqrt),
            inc=("sc_", 1))
        c.wait("vector", "sc_")
        c.op("vector", lambda v: v.tensor_scalar(
            out=rowb2[0:1, 122:138], in0=rowb2[0:1, 106:122],
            scalar1=rowb2[0:1, CR + NEGC:CR + NEGC + 1], scalar2=None, op0=A.mult),
            inc=("sv", 1))
        c.wait("tensor", "sv")
        c.op("tensor", lambda t: t.matmul(
            out=ps1[:, 16:32], lhsT=ones1[:], rhs=rowb2[0:1, 122:138],
            start=True, stop=True), inc=("st_", 1))
        c.wait("vector", "st_")
        c.op("vector", lambda v: v.tensor_copy(out=negcB[:], in_=ps1[:, 16:32]),
             inc=("sv", 1))
        attn_consts_v = c.mark("sv")
        if dbg:
            c.wait("sync", "sv")
            c.op("sync", lambda s: s.dma_start(out=DB["d_negc"], in_=rowb2[0:1, 74:138]),
                 inc=("so", 16))
            c.op("sync", lambda s: s.dma_start(out=DB["d_q8T"], in_=q8T[:]), inc=("so", 16))
            c.wait("sync", "scc", value=AG_done)
            c.op("sync", lambda s: s.dma_start(out=DB["d_k8o"], in_=k8_out.ap()), inc=("so", 16))
            c.op("sync", lambda s: s.dma_start(out=DB["d_v8o"], in_=v8_out.ap()), inc=("so", 16))
            c.op("sync", lambda s: s.dma_start(out=DB["d_v8i"], in_=v8_in.ap()), inc=("so", 16))

        if phases <= 4:
            c.wait("sync", "so")
            return
        c.wait("gpsimd", "sv")
        for b in range(2):
            c.op("gpsimd", lambda g, b=b: g.memset(
                v8h[b].rearrange("p (kc e) -> p kc e", e=129)[:, :, 128:129], 1.0),
                inc=("sq", 1))
        def head_loads(h):
            buf = h % 2
            hs = "sla" if buf == 0 else "slb_"
            for r in range(NC8):
                c.op("scalar", lambda s, h=h, r=r, buf=buf: s.dma_start(
                    out=k8h[buf][:, r * 1024:(r + 1) * 1024],
                    in_=k8_out.ap()[r * D + h * 128: r * D + (h + 1) * 128, :]),
                    inc=(hs, 16))
            for r in range(NC8):
                c.op("scalar", lambda s, h=h, r=r, buf=buf: s.dma_start(
                    out=v8h[buf].rearrange("p (kc e) -> p kc e", e=129)[:, r * 8:(r + 1) * 8, 0:128],
                    in_=v8_out.ap()[r * NH * SC + h * SC: r * NH * SC + (h + 1) * SC, :]
                    .rearrange("(j p) f -> p j f", p=P)), inc=(hs, 16))

        c.wait("scalar", "scc", value=AG_done)
        c.wait("scalar", "sq")
        c.wait("scalar", "sc_")
        head_loads(0)
        head_loads(1)
        psS = [ps1[:, 0:1024], ps2[:, 0:1024], ps1[:, 1024:2048]]
        pvt = [ps2[:, 1024 + 256 * qs: 1024 + 256 * qs + 129] for qs in range(4)]
        for h in range(NH):
            c.wait("tensor", "sla" if h % 2 == 0 else "slb_")
            for Qb in range(2):
                c.wait("tensor", "sv")
                sc_done = {}
                ex_done = {}
                pv_done = {}

                def scores_g(g):
                    par = g % 3
                    if g - 3 >= 0:
                        c.wait("tensor", "sc_", value=ex_done[g - 3])
                    for kcl in range(2):
                        kc = 2 * g + kcl
                        c.op("tensor", lambda t, h=h, Qb=Qb, kc=kc, kcl=kcl, par=par: t.matmul(
                            out=psS[par][:, kcl * 512:(kcl + 1) * 512],
                            lhsT=k8h[h % 2][:, kc * 128:(kc + 1) * 128],
                            rhs=q8T[:, h * 1024 + Qb * 512: h * 1024 + Qb * 512 + 512],
                            start=True, stop=True),
                            inc=("st_", 1) if kcl == 1 else None)
                    sc_done[g] = c.mark("st_")

                scores_g(0)
                scores_g(1)
                for g in range(32):
                    par = g % 3
                    wv = sc_done[g]
                    if g - 2 >= 0:
                        wv = max(wv, pv_done[g - 2])
                    c.wait("scalar", "st_", value=wv)
                    c.wait("scalar", "sv", value=attn_consts_v)
                    c.op("scalar", lambda s, h=h, par=par: s.activation(
                        out=PT[g % 2][:], in_=psS[par], func=AF.Exp,
                        scale=constsA[:, CEXP:CEXP + 1], bias=negcB[:, h:h + 1]),
                        inc=("sc_", 1))
                    ex_done[g] = c.mark("sc_")
                    if g + 2 < 32:
                        scores_g(g + 2)
                    c.wait("tensor", "sc_", value=ex_done[g])
                    for kcl in range(2):
                        for qs in range(4):
                            first = (g == 0 and kcl == 0)
                            c.op("tensor", lambda t, kcl=kcl, qs=qs, first=first, g=g: t.matmul(
                                out=pvt[qs],
                                lhsT=PT[g % 2][:, kcl * 512 + qs * 128: kcl * 512 + qs * 128 + 128],
                                rhs=v8h[h % 2][:, (2 * g + kcl) * 129: (2 * g + kcl) * 129 + 129],
                                start=(first and qs % 2 == 0), stop=(g == 31 and kcl == 1),
                                skip_group_check=True),
                                inc=("st_", 1) if (kcl == 1 and qs == 3) else None)
                    pv_done[g] = c.mark("st_")
                c.wait("vector", "st_")
                for qs in range(3):
                    c.op("vector", lambda v, qs=qs: v.reciprocal(
                        out=rl4[:, qs:qs + 1], in_=pvt[qs][:, 128:129]))
                c.chain("vector", lambda v: v.reciprocal(
                    out=rl4[:, 3:4], in_=pvt[3][:, 128:129]))
                for qs in range(4):
                    c.op("vector", lambda v, qs=qs: v.tensor_scalar(
                        out=attnstage[:, qs * 128:(qs + 1) * 128], in0=pvt[qs][:, 0:128],
                        scalar1=rl4[:, qs:qs + 1], scalar2=constsA[:, VSC:VSC + 1],
                        op0=A.mult, op1=A.mult), inc=("sv", 1) if qs == 3 else None)
                c.wait("tensor", "sv")
                for qs in range(4):
                    c.op("tensor", lambda t, qs=qs: t.transpose(
                        out=psT[:, qs * 128:(qs + 1) * 128],
                        in_=attnstage[:, qs * 128:(qs + 1) * 128], identity=ident16[:]),
                        inc=("st_", 1) if qs == 3 else None)
                c.wait("vector", "st_")
                c.op("vector", lambda v, h=h, Qb=Qb: v.tensor_copy(
                    out=aoT[:, h * 1024 + Qb * 512: h * 1024 + Qb * 512 + 512],
                    in_=psT[:, 0:512]), inc=("sv", 1))
            if h + 2 < NH:
                c.wait("scalar", "st_")
                head_loads(h + 2)
        if dbg:
            c.wait("sync", "sv")
            c.op("sync", lambda s: s.dma_start(out=DB["d_aoT"], in_=aoT[:]), inc=("so", 16))

        if phases <= 5:
            c.wait("sync", "so")
            return
        c.wait("sync", "sv")
        c.op("sync", lambda s: s.dma_start(
            out=rowbuf[0:1, 0:D], in_=modflat[0:1, 2 * D:3 * D]), inc=("sx", 16))
        c.wait("tensor", "sx")
        for cb in range(4):
            c.op("tensor", lambda t, cb=cb: t.matmul(
                out=ps1[:, cb * 512:(cb + 1) * 512], lhsT=ones1[:],
                rhs=rowbuf[0:1, cb * 512:(cb + 1) * 512], start=True, stop=True),
                inc=("st_", 1) if cb == 3 else None)
        c.wait("vector", "st_")
        c.op("vector", lambda v: v.tensor_copy(out=gaB[:], in_=ps1[:, 0:2048]),
             inc=("sv", 1))
        c.wait("sync", "st_")
        c.op("sync", lambda s: s.dma_start(out=rowbuf[0:1, 0:D], in_=bout), inc=("sx", 16))
        c.wait("tensor", "sx")
        c.wait("tensor", "sv")
        for cb in range(4):
            c.op("tensor", lambda t, cb=cb: t.matmul(
                out=ps1[:, cb * 512:(cb + 1) * 512], lhsT=ones1[:],
                rhs=rowbuf[0:1, cb * 512:(cb + 1) * 512], start=True, stop=True),
                inc=("st_", 1) if cb == 3 else None)
        c.op("vector", lambda v: v.tensor_scalar(
            out=opscr[:], in0=gaB[:], scalar1=0.0, scalar2=None, op0=A.is_ge))
        c.op("vector", lambda v: v.tensor_scalar(
            out=opscr[:], in0=opscr[:], scalar1=2.0, scalar2=1.0, op0=A.mult,
            op1=A.subtract))
        c.op("vector", lambda v: v.tensor_scalar(
            out=gabB[:], in0=gaB[:], scalar1=-1.0, scalar2=None, op0=A.mult))
        c.op("vector", lambda v: v.tensor_tensor(
            out=gaB[:], in0=gaB[:], in1=gabB[:], op=A.max))
        c.op("vector", lambda v: v.tensor_scalar(
            out=gaB[:], in0=gaB[:], scalar1=0.25, scalar2=None, op0=A.max))
        c.op("vector", lambda v: v.tensor_tensor(
            out=gaB[:], in0=gaB[:], in1=opscr[:], op=A.mult), inc=("sv", 1))
        c.wait("vector", "st_")
        c.op("vector", lambda v: v.tensor_tensor(
            out=gabB[:], in0=gaB[:], in1=ps1[:, 0:2048], op=A.mult), inc=("sv", 1))
        gab_ready = c.mark("sv")

        wo4 = wout.rearrange("(hc p) o -> hc p o", p=P)
        units = [(ob, sh) for ob in range(4) for sh in range(2)]
        banks = [ps1[:, i * 512:(i + 1) * 512] for i in range(4)] + \
                [ps2[:, i * 512:(i + 1) * 512] for i in range(4)]
        c.wait("tensor", "sv")
        c.wait("gpsimd", "st_")
        c.wait("sync", "st_")
        svo = c.mark("sv")
        wo_ld = {}
        wo_mm = {}
        for i, (ob, sh) in enumerate(units):
            par = i % 2
            bgrp = banks[0:4] if par == 0 else banks[4:8]
            for hc in range(8):
                k = i * 16 + hc
                if k - 8 in wo_mm:
                    c.wait("gpsimd", "st_", value=wo_mm[k - 8])
                c.op("gpsimd", lambda g, ob=ob, hc=hc: g.dma_start(
                    out=woutb8[hc % 8][:],
                    in_=wo4[hc][:, ob * 512:(ob + 1) * 512]), inc=(f"sw{hc % 8}", 16))
            if i >= 2:
                c.wait("tensor", "sv", value=svo + (i - 1) * 4)
            for hc in range(NH):
                k = i * 16 + hc
                c.wait("tensor", f"sw{hc % 8}")
                for sc2 in range(4):
                    sc = sh * 4 + sc2
                    c.op("tensor", lambda t, hc=hc, sc=sc, sc2=sc2, bgrp=bgrp: t.matmul(
                        out=bgrp[sc2],
                        lhsT=aoT[:, hc * 1024 + sc * 128: hc * 1024 + sc * 128 + 128],
                        rhs=woutb8[hc % 8][:],
                        start=(hc == 0), stop=(hc == 15)),
                        inc=("st_", 1) if sc2 == 3 else None)
                wo_mm[k] = c.mark("st_")
                if hc + 8 < NH:
                    c.wait("gpsimd", "st_", value=wo_mm[k])
                    c.op("gpsimd", lambda g, ob=ob, hc=hc: g.dma_start(
                        out=woutb8[(hc + 8) % 8][:],
                        in_=wo4[hc + 8][:, ob * 512:(ob + 1) * 512]),
                        inc=(f"sw{(hc + 8) % 8}", 16))
            for sc2 in range(4):
                sc = sh * 4 + sc2
                c.op("sync", lambda s, sc=sc, ob=ob, sc2=sc2: s.dma_start(
                    out=xsl[sc2 % 2][:],
                    in_=x_in[sc * 128:(sc + 1) * 128, ob * 512:(ob + 1) * 512]),
                    inc=("sx", 16))
                c.wait("vector", "st_")
                c.wait("vector", "sx", lag=0)
                c.wait("vector", "sv", value=gab_ready)
                c.op("vector", lambda v, sc2=sc2, ob=ob, bgrp=bgrp: v.tensor_tensor(
                    out=epis[:], in0=bgrp[sc2], in1=gaB[:, ob * 512:(ob + 1) * 512],
                    op=A.mult))
                c.op("vector", lambda v, ob=ob: v.tensor_tensor(
                    out=epis[:], in0=epis[:], in1=gabB[:, ob * 512:(ob + 1) * 512],
                    op=A.add))
                c.op("vector", lambda v, sc2=sc2: v.tensor_tensor(
                    out=x2b[sc2 % 2][:], in0=epis[:], in1=xsl[sc2 % 2][:], op=A.add),
                    inc=("sv", 1))
                c.wait("sync", "sv")
                c.op("sync", lambda s, sc=sc, ob=ob, sc2=sc2: s.dma_start(
                    out=x2_sc.ap()[sc * 128:(sc + 1) * 128, ob * 512:(ob + 1) * 512],
                    in_=x2b[sc2 % 2][:]), inc=("sx", 16))
        x2_done_sx = c.mark("sx")
        if dbg:
            c.wait("sync", "sx", value=x2_done_sx)
            c.op("sync", lambda s: s.dma_start(out=DB["d_x2"], in_=x2_sc.ap()),
                 inc=("so", 16))

        c.wait("sync", "sx", value=x2_done_sx)
        emit_LN(lambda sc: x2_sc.ap()[sc * 128:(sc + 1) * 128, :], cb1_c, sb_c,
                tap=DB["d_h2T"] if dbg else None)
        h2T_done = c.mark("sc_")

        if phases <= 6:
            c.wait("sync", "so")
            return
        gw4 = gw.rearrange("(dc p) (ic f) -> ic p dc f", p=P, f=128)
        uw4 = uw.rearrange("(dc p) (ic f) -> ic p dc f", p=P, f=128)
        c.wait("gpsimd", "sx", value=x2_done_sx)
        c.wait("gpsimd", "sba")
        c.wait("gpsimd", "sbb")
        c.wait("gpsimd", "sc_", value=h2T_done)
        gu_sems = ["sea", "seb", "sla"]
        for ic in range(3):
            sj = gu_sems[ic % 3]
            c.op("gpsimd", lambda g, ic=ic: g.dma_start(
                out=gwb3[ic % 3][:], in_=g8_sc.ap()[ic]), inc=(sj, 16))
            c.op("gpsimd", lambda g, ic=ic: g.dma_start(
                out=uwb3[ic % 3][:], in_=u8_sc.ap()[ic]), inc=(sj, 16))
        c.wait("tensor", "sc_", value=h2T_done)
        svf = c.mark("sv")
        ffn_mm = {}
        for ic in range(48):
            reg = ps1 if ic % 2 == 0 else ps2
            c.wait("tensor", gu_sems[ic % 3])
            if ic >= 2:
                c.wait("tensor", "sv", value=svf + (ic - 2) + 1)
            for dc in range(16):
                for sb2 in range(2):
                    c.op("tensor", lambda t, ic=ic, dc=dc, sb2=sb2, reg=reg: t.matmul(
                        out=reg[:, sb2 * 512:(sb2 + 1) * 512],
                        lhsT=gwb3[ic % 3][:, dc * 128:(dc + 1) * 128],
                        rhs=hT[:, dc * 1024 + sb2 * 512: dc * 1024 + sb2 * 512 + 512],
                        start=(dc == 0), stop=(dc == 15)))
                for sb2 in range(2):
                    c.op("tensor", lambda t, ic=ic, dc=dc, sb2=sb2, reg=reg: t.matmul(
                        out=reg[:, 1024 + sb2 * 512: 1024 + (sb2 + 1) * 512],
                        lhsT=uwb3[ic % 3][:, dc * 128:(dc + 1) * 128],
                        rhs=hT[:, dc * 1024 + sb2 * 512: dc * 1024 + sb2 * 512 + 512],
                        start=(dc == 0), stop=(dc == 15)),
                        inc=("st_", 1) if (dc == 15 and sb2 == 1) else None)
            ffn_mm[ic] = c.mark("st_")
            if ic + 3 < 48:
                c.wait("gpsimd", "st_", value=ffn_mm[ic])
                sj = gu_sems[ic % 3]
                c.op("gpsimd", lambda g, ic=ic: g.dma_start(
                    out=gwb3[ic % 3][:], in_=g8_sc.ap()[ic + 3]), inc=(sj, 16))
                c.op("gpsimd", lambda g, ic=ic: g.dma_start(
                    out=uwb3[ic % 3][:], in_=u8_sc.ap()[ic + 3]), inc=(sj, 16))
            c.wait("scalar", "st_")
            c.wait("scalar", "sv", lag=0)
            c.op("scalar", lambda s, ic=ic, reg=reg: s.activation(
                out=sgbuf[:], in_=reg[:, 0:1024], func=AF.Silu,
                scale=gscol[:, ic:ic + 1]), inc=("sc_", 1))
            c.wait("vector", "sc_")
            c.op("vector", lambda v, ic=ic, reg=reg: v.scalar_tensor_tensor(
                out=fuseT[:, ic * 1024:(ic + 1) * 1024], in0=reg[:, 1024:2048],
                scalar=uscol[:, ic:ic + 1], in1=sgbuf[:], op0=A.mult, op1=A.mult),
                inc=("sv", 1))
        fuse_done = c.mark("sv")

        c.op("sync", lambda s: s.dma_start(
            out=rowbuf[0:1, 0:D], in_=modflat[0:1, 5 * D:6 * D]), inc=("sx", 16))
        c.wait("tensor", "sx")
        c.wait("tensor", "sv", value=fuse_done)
        for cb in range(4):
            c.op("tensor", lambda t, cb=cb: t.matmul(
                out=ps1[:, cb * 512:(cb + 1) * 512], lhsT=ones1[:],
                rhs=rowbuf[0:1, cb * 512:(cb + 1) * 512], start=True, stop=True),
                inc=("st_", 1) if cb == 3 else None)
        c.wait("vector", "st_")
        c.op("vector", lambda v: v.tensor_copy(out=dsb2B[:], in_=ps1[:, 0:2048]),
             inc=("sv", 1))
        c.wait("sync", "st_")
        c.op("sync", lambda s: s.dma_start(out=rowbuf[0:1, 0:D], in_=dsv), inc=("sx", 16))
        c.wait("tensor", "sx")
        c.wait("tensor", "sv")
        for cb in range(4):
            c.op("tensor", lambda t, cb=cb: t.matmul(
                out=ps1[:, cb * 512:(cb + 1) * 512], lhsT=ones1[:],
                rhs=rowbuf[0:1, cb * 512:(cb + 1) * 512], start=True, stop=True),
                inc=("st_", 1) if cb == 3 else None)
        c.op("vector", lambda v: v.tensor_scalar(
            out=dscr[:], in0=dsb2B[:], scalar1=0.0, scalar2=None, op0=A.is_ge))
        c.op("vector", lambda v: v.tensor_scalar(
            out=dscr[:], in0=dscr[:], scalar1=2.0, scalar2=1.0, op0=A.mult,
            op1=A.subtract))
        c.op("vector", lambda v: v.tensor_scalar(
            out=fx2all[:, 0:2048], in0=dsb2B[:], scalar1=-1.0, scalar2=None, op0=A.mult))
        c.op("vector", lambda v: v.tensor_tensor(
            out=dsb2B[:], in0=dsb2B[:], in1=fx2all[:, 0:2048], op=A.max))
        c.op("vector", lambda v: v.tensor_scalar(
            out=dsb2B[:], in0=dsb2B[:], scalar1=0.25, scalar2=None, op0=A.max))
        c.op("vector", lambda v: v.tensor_tensor(
            out=dsb2B[:], in0=dsb2B[:], in1=dscr[:], op=A.mult))
        c.wait("vector", "st_")
        c.op("vector", lambda v: v.tensor_tensor(
            out=dsb2B[:], in0=dsb2B[:], in1=ps1[:, 0:2048], op=A.mult), inc=("sv", 1))
        dsb2_ready = c.mark("sv")

        dw4 = dw.rearrange("(ic p) o -> ic p o", p=P)
        dw_sems = ["sw0", "sw1", "sw2", "sw3", "sw4", "sw5"]
        dmm = {}
        ep_mk = {}
        for m in range(6):
            c.op("gpsimd", lambda g, m=m: g.dma_start(
                out=dwb6[m % 6][:], in_=d8_sc.ap()[m % 48][:, (m // 48) * 512:(m // 48) * 512 + 512]),
                inc=(dw_sems[m % 6], 16))
        c.wait("tensor", "sv", value=dsb2_ready)
        sob = c.mark("so")
        for ob in range(4):
            c.wait("sync", "st_")
            c.wait("sync", "sv")
            sxd = c.mark("sx")
            for sc in range(8):
                c.op("sync", lambda s, sc=sc, ob=ob: s.dma_start(
                    out=fx2all[:, sc * 512:(sc + 1) * 512],
                    in_=x2_sc.ap()[sc * 128:(sc + 1) * 128, ob * 512:(ob + 1) * 512]),
                    inc=("sx", 16))
            for ic in range(48):
                m = ob * 48 + ic
                c.wait("tensor", dw_sems[m % 6])
                for sc in range(8):
                    if ob > 0 and ic == 0:
                        c.wait("tensor", "sv", value=ep_mk[sc])
                    c.op("tensor", lambda t, ic=ic, sc=sc, m=m: t.matmul(
                        out=banks[sc],
                        lhsT=fuseT[:, ic * 1024 + sc * 128: ic * 1024 + sc * 128 + 128],
                        rhs=dwb6[m % 6][:],
                        start=(ic == 0), stop=(ic == 47)),
                        inc=("st_", 1) if sc == 7 else None)
                dmm[m] = c.mark("st_")
                if m + 6 < 192:
                    m2 = m + 6
                    c.wait("gpsimd", "st_", value=dmm[m])
                    c.op("gpsimd", lambda g, m2=m2: g.dma_start(
                        out=dwb6[m2 % 6][:],
                        in_=d8_sc.ap()[m2 % 48][:, (m2 // 48) * 512:(m2 // 48) * 512 + 512]),
                        inc=(dw_sems[m2 % 6], 16))
            c.wait("vector", "st_")
            c.wait("vector", "sx", value=sxd + 16 * 8)
            c.wait("vector", "sv", value=dsb2_ready)
            for sc in range(8):
                n_st = ob * 8 + sc
                if n_st >= 2:
                    c.wait("vector", "so", value=sob + 16 * (n_st - 1))
                c.op("vector", lambda v, sc=sc, ob=ob: v.tensor_tensor(
                    out=fepi[sc % 2][:], in0=banks[sc],
                    in1=dsb2B[:, ob * 512:(ob + 1) * 512], op=A.mult))
                c.op("vector", lambda v, sc=sc: v.tensor_tensor(
                    out=fepi[sc % 2][:], in0=fepi[sc % 2][:],
                    in1=fx2all[:, sc * 512:(sc + 1) * 512], op=A.add), inc=("sv", 1))
                ep_mk[sc] = c.mark("sv")
                c.wait("sync", "sv")
                c.op("sync", lambda s, sc=sc, ob=ob: s.dma_start(
                    out=y_out[sc * 128:(sc + 1) * 128, ob * 512:(ob + 1) * 512],
                    in_=fepi[sc % 2][:]), inc=("so", 16))
        if dbg:
            c.op("sync", lambda s: s.dma_start(out=DB["d_v8late"], in_=v8_out.ap()),
                 inc=("so", 16))
        c.wait("sync", "so")

    with nc.Block() as block:
        @block.sync
        def _(eng):
            script(Ctx("sync", eng, sems))

        @block.scalar
        def _(eng):
            script(Ctx("scalar", eng, sems))

        @block.gpsimd
        def _(eng):
            with allow_ici_all_to_all():
                script(Ctx("gpsimd", eng, sems))

        @block.vector
        def _(eng):
            script(Ctx("vector", eng, sems))

        @block.tensor
        def _(eng):
            script(Ctx("tensor", eng, sems))

    stack.close()
    return nc


def make_in_maps(inputs):
    x = np.ascontiguousarray(inputs["L0_attn_x"].reshape(NC8 * SC, D))
    wqkv = np.ascontiguousarray(inputs["L0_attn_W_qkv"])
    wout = np.ascontiguousarray(inputs["L0_attn_W_out"])
    bout = np.ascontiguousarray(inputs["L0_attn_b_out"]).reshape(1, D)
    gw = np.ascontiguousarray(inputs["L0_ffn_gate_w"])
    uw = np.ascontiguousarray(inputs["L0_ffn_up_w"])
    dw = np.ascontiguousarray(inputs["L0_ffn_down_w"])
    gsv = np.ascontiguousarray(inputs["L0_ffn_gate_scale"]).reshape(1, FF)
    usv = np.ascontiguousarray(inputs["L0_ffn_up_scale"]).reshape(1, FF)
    dsv = np.ascontiguousarray(inputs["L0_ffn_down_scale"]).reshape(1, D)
    cvec = np.ascontiguousarray(inputs["L0_c"]).reshape(1, D)
    wmod = np.ascontiguousarray(inputs["L0_wmod"])
    maps = []
    for r in range(NC8):
        maps.append({
            "x": np.ascontiguousarray(x[r * SC:(r + 1) * SC]),
            "wqkv": wqkv, "wout": wout, "bout": bout,
            "gw": gw, "uw": uw, "dw": dw, "gsv": gsv, "usv": usv, "dsv": dsv,
            "cvec": cvec,
            "wmod": np.ascontiguousarray(wmod[:, r * 1536:(r + 1) * 1536]),
        })
    return maps


def kernel(**inputs):
    nc = build_nc(dbg=False)
    maps = make_in_maps(inputs)
    res = run_bass_kernel_spmd(nc, maps, core_ids=list(range(NC8)))
    y = np.concatenate([res.results[r]["y"] for r in range(NC8)], axis=0)
    return y.reshape(1, NC8 * SC, D)
